# version 07
# baseline (speedup 1.0000x reference)
import math
import contextlib
import numpy as np
import concourse.bass as bass
import concourse.mybir as mybir
from concourse.bass import broadcast_tensor_aps
from concourse.bass_utils import run_bass_kernel_spmd

F32 = mybir.dt.float32
BF16 = mybir.dt.bfloat16
I8 = mybir.dt.int8
AF = mybir.ActivationFunctionType
ALU = mybir.AluOpType
AX = mybir.AxisListType

T = 4096
NT = 32
D = 1024
H = 16
HD = 64
NI = 4096
NCHIP = 8
INV_N = 1.0 / 4194304.0
LN4096 = math.log(4096.0)

NCH = 16
NW = 3
NQC = NW * NCH

MEAN_TILE = [min(31, int(j / 1.55)) for j in range(NQC)]
FIN_TILE = [MEAN_TILE[NCH - 1] + 1, MEAN_TILE[2 * NCH - 1] + 1, 31]
QSLOT = {j: 33 + (2 * j) // 3 for j in range(NQC)}
P1_QUANT = {i: [] for i in range(32)}
P2A_QUANT = []
P2B_QUANT = {i: [j for j, s in QSLOT.items() if s == 33 + i] for i in range(32)}
P3A_QUANT = {i: [] for i in range(32)}
N_P1Q = 0


P2B_LVL = 4
P3A_DEPTH = 4
P3A_GB = [0, 1, 4, 5]
P3A_UB = [2, 3, 6, 7]
P3B_SHIFT = False


def build(upto=5, dbg=False):
    nc = bass.Bass("TRN2", target_bir_lowering=False)
    lvl = P2B_LVL if upto == 3 else 4
    scratch_kind = "ExternalOutput" if dbg else "Internal"

    x_h = nc.dram_tensor("x", [T, D], F32, kind="ExternalInput").ap()
    wq_h = nc.dram_tensor("wq", [D, D], F32, kind="ExternalInput").ap()
    wk_h = nc.dram_tensor("wk", [D, D], F32, kind="ExternalInput").ap()
    wv_h = nc.dram_tensor("wv", [D, D], F32, kind="ExternalInput").ap()
    wo_h = nc.dram_tensor("wo", [D, D], F32, kind="ExternalInput").ap()
    bq_h = nc.dram_tensor("bq", [D], F32, kind="ExternalInput").ap()
    bk_h = nc.dram_tensor("bk", [D], F32, kind="ExternalInput").ap()
    bv_h = nc.dram_tensor("bv", [D], F32, kind="ExternalInput").ap()
    bo_h = nc.dram_tensor("bo", [D], F32, kind="ExternalInput").ap()
    m_h = nc.dram_tensor("m", [H], F32, kind="ExternalInput").ap()
    wg_h = nc.dram_tensor("wg", [NI, D], F32, kind="ExternalInput").ap()
    wu_h = nc.dram_tensor("wu", [NI, D], F32, kind="ExternalInput").ap()
    wd_h = nc.dram_tensor("wd", [D, NI], F32, kind="ExternalInput").ap()
    out_h = nc.dram_tensor("out", [T, D], F32, kind="ExternalOutput").ap()

    xbf = nc.dram_tensor("xbf", [T, D], BF16).ap()
    xqsc = nc.dram_tensor("xqsc", [T, D], BF16).ap()
    qsc = nc.dram_tensor("qsc", [T, D], BF16, kind=scratch_kind).ap()
    xqT_sc = nc.dram_tensor("xqT_sc", [8, 128, T], BF16, kind=scratch_kind).ap()
    x1sc = nc.dram_tensor("x1sc", [T, D], F32, kind=scratch_kind).ap()
    hqsc = nc.dram_tensor("hqsc", [T, NI], BF16, kind=scratch_kind).ap()
    wgq = nc.dram_tensor("wgq", [NI, D], BF16, kind=scratch_kind).ap()
    wuq = nc.dram_tensor("wuq", [NI, D], BF16).ap()
    wdq = nc.dram_tensor("wdq", [D, NI], BF16).ap()
    mrsc = nc.dram_tensor("mrsc", [NW, 128], F32).ap()
    if dbg:
        dbg_kv = nc.dram_tensor("dbg_kv", [128, D], BF16,
                                kind="ExternalOutput").ap()
        dbg_ofm = nc.dram_tensor("dbg_ofm", [128, 8, T], BF16,
                                 kind="ExternalOutput").ap()
        dbg_macc = nc.dram_tensor("dbg_macc", [128, 51], F32,
                                  kind="ExternalOutput").ap()
        dbg_swuw = nc.dram_tensor("dbg_swuw", [128, 8], F32,
                                  kind="ExternalOutput").ap()
        dbg_bst = nc.dram_tensor("dbg_bst", [1, 64], F32,
                                 kind="ExternalOutput").ap()
    wq_src = [wg_h, wu_h, wd_h]
    wq_dst = [wgq, wuq, wdq]

    def wchunk(ap, j):
        r, c = ap.shape
        v = ap.rearrange("(a k) n -> k a n", k=128)
        if c >= 2048:
            n2 = c // 2048
            return v[:, j // n2, (j % n2) * 2048:(j % n2) * 2048 + 2048], "2d"
        na = 2048 // c
        return v[:, j * na:(j + 1) * na, :], "3d"

    with (
        nc.sbuf_tensor("actT", [128, 8, T], BF16) as actT,
        nc.sbuf_tensor("actU", [128, 8, T], BF16) as actU,
        nc.sbuf_tensor("bias_b", [128, 4, D], BF16) as bias_b,
        nc.sbuf_tensor("kvbf", [128, D], BF16) as kvbf,
        nc.sbuf_tensor("rsc", [128, H], F32) as rsc,
        nc.sbuf_tensor("sxs", [128, NT], F32) as sxs,
        nc.sbuf_tensor("sds", [128, NT], F32) as sds,
        nc.sbuf_tensor("swuw", [128, 8], F32) as swuw,
        nc.sbuf_tensor("cst", [128, 4], F32) as cst,
        nc.sbuf_tensor("ones", [1, 128], F32) as ones,
        nc.sbuf_tensor("onesb", [1, 128], BF16) as onesb,
        nc.sbuf_tensor("bst", [1, 1280], F32) as bst,
        nc.sbuf_tensor("idn", [128, 128], BF16) as idn,
        nc.sbuf_tensor("arena", [128, 16160], F32) as arena,
        nc.psum_tensor("psA", [128, 4096], F32) as psA,
        nc.Block() as block,
    ):
        def af32(off, n):
            assert off + n <= 16160, (off, n)
            return arena[:, off:off + n]

        def abf(off, nf32):
            return af32(off, nf32).bitcast(BF16)

        def ai8(off, nf32):
            return af32(off, nf32).bitcast(I8)

        qf = [af32(0, 1024), af32(1024, 1024)]
        kf = [af32(2048, 1024), af32(3072, 1024)]
        sqq = af32(4096, 1024)
        sqk = af32(5120, 1024)
        qn = [abf(6144, 512), abf(6656, 512)]
        kn = [abf(7168, 512), abf(7680, 512)]
        vb = [abf(8192, 512), abf(8704, 512)]
        rq = [af32(9216, 16), af32(9232, 16)]
        rk = [af32(9248, 16), af32(9264, 16)]
        srtq = [af32(9280, 16), af32(9296, 16)]
        srtk = [af32(9312, 16), af32(9328, 16)]
        nrm = af32(9344, 16)
        rinvq = af32(9360, 16)
        rinvk = [af32(9376, 16), af32(9392, 16)]
        macc = af32(9408, 48)
        mred = af32(9456, 3)
        wbuf = [af32(9472, 2048), af32(11520, 2048)]
        i8q = ai8(13568, 512)
        wqbf = [abf(14080, 1024)]

        def wsb(j):
            ap, kind = wchunk(wq_src[j // NCH], j % NCH)
            if kind == "2d":
                return ap, wbuf[j % 2]
            return ap, wbuf[j % 2].rearrange("p (a b) -> p a b", b=1024)

        def wsb_out(j):
            ap, kind = wchunk(wq_dst[j // NCH], j % NCH)
            if kind == "2d":
                return ap, wqbf[0]
            return ap, wqbf[0].rearrange("p (a b) -> p a b", b=1024)

        xres = [af32(0, 1024), af32(1024, 1024)]
        z2 = af32(2048, 1024)
        sq2 = af32(3072, 1024)
        x1b = [af32(4096, 1024), af32(5120, 1024)]
        xqbf = [abf(6144, 512), abf(6656, 512)]
        xi8 = ai8(7168, 256)
        st2 = af32(8448, 32)
        xqTt = [abf(7424, 512), abf(7936, 512)]

        def st2c(k):
            return st2[:, k:k + 1]

        gbuf = af32(1024, 4096)
        hbuf = af32(5120, 4096)
        hqb = [abf(9216, 2048), abf(11264, 2048)]
        hi8 = ai8(13312, 1024)
        st3 = af32(14336, 32)
        sgu = af32(14368, 4)
        xqTi = [abf(15104, 512), abf(15616, 512)]

        def st3c(k):
            return st3[:, k:k + 1]

        hqi = [abf(0, 2048), abf(2048, 2048)]
        hqT = [abf(4096, 2048), abf(6144, 2048)]
        x1i = [af32(8192, 1024), af32(9216, 1024)]
        z3 = af32(10240, 1024)
        sq3 = af32(11264, 1024)
        outb = [af32(12288, 1024), af32(13312, 1024)]
        st4 = af32(16128, 32)

        def st4c(k):
            return st4[:, k:k + 1]

        actTf = actT[:].rearrange("p a b -> p (a b)")

        def bank(b):
            return psA[:, b * 512:(b + 1) * 512]

        def bankbf3(b):
            return psA[:, b * 512:(b + 1) * 512].bitcast(BF16).rearrange(
                "p (a b) -> p a b", b=128)

        semnames = [
            "dBias", "dWm", "dXT", "dWq", "dQT", "dXin", "dWgT", "dWuT",
            "dXQT", "dWdT", "dHQ", "dX1", "gLoad", "sMR",
            "sQN", "sWQ", "sX1", "sXQT", "sHQ", "sOUT", "sXQ", "dXB", "sXB",
            "tBias", "tRsc", "tSum", "tSWb", "tQ", "tK", "tV", "tKV",
            "tO", "tZ", "tXQT", "tG", "tU", "tTr", "tD",
            "vBias", "vSig", "vRin", "vRsc", "vMean", "vMred", "vSWr",
            "vSW", "vQe", "vRQ", "vKe", "vRK", "vVB", "vQN", "vRKi",
            "vKV", "vOev", "vZ1", "vVar2", "vX1", "vXI8", "vXQTe",
            "vSG", "vUev", "vSH", "vHI8", "vHTe", "vDev", "vVar3", "vOut",
            "aSig", "aExp", "aSQq", "aRTq", "aSQk", "aRTk", "aSQ2",
            "aRS2", "aGev", "aHTe", "aSQ3", "aRS3", "aWM", "aHM", "aXQe",
            "pConst", "pKN", "pZ", "pXQ", "pQbf", "pHm", "pHQ", "sDBG",
            "vFen",
        ]
        stack = contextlib.ExitStack()
        S = {n: stack.enter_context(nc.semaphore(n)) for n in semnames}

        def wge(eng, name, v):
            if v > 0:
                eng.wait_ge(S[name], v)

        @block.sync
        def _(sync):
            for j, bh in enumerate([bq_h, bk_h, bv_h, bo_h]):
                wge(sync, "tBias", 2 * j)
                sync.dma_start(out=bst[0:1, 0:D],
                               in_=bh.rearrange("(o n) -> o n", o=1)
                               ).then_inc(S["dBias"], 16)
            sync.dma_start(out=bst[0:1, 1024:1024 + H],
                           in_=m_h.rearrange("(o n) -> o n", o=1)
                           ).then_inc(S["dBias"], 16)
            for c in range(8):
                sync.wait_ge(S["gLoad"], 16 * (c + 1))
                sync.dma_start(out=actT[:, c, :],
                               in_=xbf[:, c * 128:(c + 1) * 128],
                               transpose=True).then_inc(S["dXT"], 16)

            def qT_batch(qt):
                wge(sync, "tV", 8 * (qt + 1) if qt < 3 else NT)
                sync.wait_ge(S["sQN"], 16 * 8 * (qt + 1))
                for c in range(8):
                    sync.dma_start(out=actT[:, c, qt * 1024:(qt + 1) * 1024],
                                   in_=qsc[qt * 1024:(qt + 1) * 1024,
                                           c * 128:(c + 1) * 128],
                                   transpose=True).then_inc(S["dQT"], 16)

            QT_AT = {15: 0, 27: 1, 40: 2}
            for j in range(NQC):
                wge(sync, "vMean", j - 1)
                src, dst = wsb(j)
                sync.dma_start(out=dst, in_=src).then_inc(S["dWm"], 16)
                if upto >= 2 and j in QT_AT:
                    qT_batch(QT_AT[j])
            if upto < 2:
                return
            qT_batch(3)
            if upto < 3:
                return

            def wgT_batch(qt):
                wge(sync, "tZ", 8 * (qt + 1) if qt < 3 else NT)
                if dbg:
                    sync.wait_ge(S["sDBG"], 32)
                sync.wait_ge(S["sWQ"], 16 * (4 * qt + 4))
                for c in range(8):
                    sync.dma_start(out=actT[:, c, qt * 1024:(qt + 1) * 1024],
                                   in_=wgq[qt * 1024:(qt + 1) * 1024,
                                           c * 128:(c + 1) * 128],
                                   transpose=True).then_inc(S["dWgT"], 16)

            def wuT_batch(qt):
                if qt == 3:
                    sync.wait_ge(S["tZ"], NT)
                sync.wait_ge(S["sWQ"], 16 * (16 + 4 * qt + 4))
                for c in range(8):
                    sync.dma_start(out=actU[:, c, qt * 1024:(qt + 1) * 1024],
                                   in_=wuq[qt * 1024:(qt + 1) * 1024,
                                           c * 128:(c + 1) * 128],
                                   transpose=True).then_inc(S["dWuT"], 16)

            def xqTi_load(i):
                wge(sync, "sXQT", 16 * (i + 1))
                wge(sync, "tU", 8 * (i - 1))
                sync.dma_start(out=xqTi[i % 2].rearrange("p (a b) -> p a b", b=128),
                               in_=xqT_sc.transpose([1, 0, 2])[:, :, i * 128:(i + 1) * 128]
                               ).then_inc(S["dXQT"], 16)

            for i in range(NT):
                wge(sync, "vZ1", i - 1)
                sync.dma_start(out=xres[i % 2],
                               in_=x_h[i * 128:(i + 1) * 128, :]
                               ).then_inc(S["dXin"], 16)
                if upto >= 4:
                    if i == 9:
                        wgT_batch(0)
                    elif i == 10:
                        xqTi_load(0)
                    elif i == 12:
                        xqTi_load(1)
                    elif i == 13:
                        wuT_batch(0)
                    elif i == 17:
                        wgT_batch(1)
                    elif i == 19:
                        wuT_batch(1)
                    elif i == 25:
                        wgT_batch(2)
                    elif i == 27:
                        wuT_batch(2)
            if upto < 4:
                return
            wgT_batch(3)
            wuT_batch(3)
            for i in range(2, NT):
                xqTi_load(i)
            if upto < 5:
                return
            def hqT_tr(i):
                if i == 0:
                    sync.wait_ge(S["vHI8"], NT)
                sync.wait_ge(S["sHQ"], 16 * (i + 1))
                wge(sync, "tD", 2 * (i - 1))
                hv = hqT[i % 2].rearrange("p (a b) -> p a b", b=128)
                for c in range(32):
                    sync.dma_start(out=hv[:, c, :],
                                   in_=hqsc[i * 128:(i + 1) * 128,
                                            c * 128:(c + 1) * 128],
                                   transpose=True).then_inc(S["dHQ"], 16)

            sync.wait_ge(S["tU"], 8 * NT)
            for c in range(32):
                sync.dma_start(out=actTf[:, c * 1024:(c + 1) * 1024],
                               in_=wdq[:, c * 128:(c + 1) * 128],
                               transpose=True).then_inc(S["dWdT"], 16)
                if c == 7:
                    hqT_tr(0)
                elif c == 15:
                    hqT_tr(1)
            for i in range(2, NT):
                hqT_tr(i)

        @block.gpsimd
        def _(g):
            for c in range(8):
                g.dma_start(out=xbf[:, c * 128:(c + 1) * 128],
                            in_=x_h[:, c * 128:(c + 1) * 128]
                            ).then_inc(S["gLoad"], 16)
            for j, wh in enumerate([wq_h, wk_h, wv_h, wo_h]):
                g.dma_start(out=actU[:, :, j * 1024:(j + 1) * 1024],
                            in_=wh.rearrange("(c k) n -> k c n", k=128)
                            ).then_inc(S["gLoad"], 16)
            g.memset(ones[:], 1.0)
            g.memset(onesb[:], 1.0)
            g.memset(cst[:, 0:1], 127.0)
            g.memset(cst[:, 1:2], 1e-5)
            g.memset(cst[:, 2:3], 1e-12)
            g.memset(cst[:, 3:4], 1.0)
            g.memset(idn[:], 0.0)
            g.affine_select(out=idn[:], in_=idn[:], compare_op=ALU.not_equal,
                            fill=1.0, base=0, pattern=[[-1, 128]],
                            channel_multiplier=1).then_inc(S["pConst"], 1)

            if upto < 3:
                return
            for j in range(NQC):
                if j < 2:
                    g.wait_ge(S["vMean"], NQC)
                else:
                    wge(g, "pQbf", j - 1)
                src, dst = wsb(j)
                g.dma_start(out=dst, in_=src).then_inc(S["dWq"], 16)

        @block.tensor
        def _(t):
            t.wait_ge(S["pConst"], 1)
            for j in range(8):
                t.wait_ge(S["dBias"], 16 * (j // 2 + 1))
                wge(t, "vBias", j - 1)
                t.matmul(bank(j % 2), ones[0:1, :],
                         bst[0:1, (j % 2) * 512:(j % 2) * 512 + 512],
                         start=True, stop=True,
                         skip_group_check=True).then_inc(S["tBias"], 1)
            t.wait_ge(S["vRin"], 1)
            t.matmul(psA[:, 3584:3584 + H], ones[0:1, :],
                     bst[0:1, 1088:1088 + H], start=True, stop=True,
                     skip_group_check=True).then_inc(S["tRsc"], 1)

            t.wait_ge(S["vBias"], 8)
            t.wait_ge(S["vRsc"], 1)
            if upto == 0:
                for w in range(NW):
                    wge(t, "vSW", w)
                    wge(t, "vSWr", w + 1)
                    t.matmul(psA[:, 1540 + 2 * w:1542 + 2 * w], ones[0:1, :],
                             bst[0:1, 1120 + 2 * w:1122 + 2 * w],
                             start=True, stop=True,
                             skip_group_check=True).then_inc(S["tSWb"], 1)
                return
            def kv_mms(i):
                t.wait_ge(S["pKN"], i + 1)
                t.wait_ge(S["vVB"], i + 1)
                ins = None
                for h in range(H):
                    ins = t.matmul(
                        psA[:, 3072 + 64 * h:3072 + 64 * h + 64],
                        kn[i % 2][:, (h // 2) * 128:(h // 2) * 128 + 128],
                        vb[i % 2][:, h * HD:(h + 1) * HD],
                        start=(i == 0 and h in (0, 8)),
                        stop=(i == NT - 1 and h == H - 1),
                        skip_group_check=True)
                ins.then_inc(S["tKV"], 1)

            for i in range(NT):
                for mi, (base, gate, sem) in enumerate(
                        [(0, "vQe", "tQ"), (1024, "vKe", "tK"),
                         (2048, "vVB", "tV")]):
                    wge(t, gate, i)
                    if i == 0:
                        t.wait_ge(S["gLoad"], 16 * (9 + mi))
                    ins = None
                    for c in range(8):
                        if i == 0 and mi == 0:
                            t.wait_ge(S["dXT"], 16 * (c + 1))
                        for nh in range(2):
                            ins = t.matmul(
                                psA[:, base + nh * 512: base + nh * 512 + 512],
                                actT[:, c, i * 128:(i + 1) * 128],
                                actU[:, c, mi * 1024 + nh * 512:
                                     mi * 1024 + nh * 512 + 512],
                                start=(c == 0), stop=(c == 7),
                                skip_group_check=True)
                    ins.then_inc(S[sem], 1)
                if i >= 1:
                    kv_mms(i - 1)
            kv_mms(NT - 1)
            t.wait_ge(S["vKe"], NT)
            for w in range(NW):
                wge(t, "vSW", w)
                wge(t, "vSWr", w + 1)
                t.matmul(psA[:, 1540 + 2 * w:1542 + 2 * w], ones[0:1, :],
                         bst[0:1, 1120 + 2 * w:1122 + 2 * w],
                         start=True, stop=True,
                         skip_group_check=True).then_inc(S["tSWb"], 1)

            if upto < 2:
                return
            t.wait_ge(S["vKV"], 1)
            t.wait_ge(S["vQe"], NT)
            t.wait_ge(S["vSW"], NW)
            for s in range(8):
                t.wait_ge(S["dQT"], 128 * (s // 2 + 1))
                for c in range(8):
                    cs = 8 * s + c
                    wge(t, "vOev", cs - 3)
                    ins = None
                    for h in (2 * c, 2 * c + 1):
                        r0 = 64 * (h % 2)
                        ins = t.matmul(
                            psA[r0:r0 + 64, (cs % 4) * 512:(cs % 4) * 512 + 512],
                            kvbf[r0:r0 + 64, h * HD:(h + 1) * HD],
                            actT[r0:r0 + 64, c, s * 512:(s + 1) * 512],
                            start=True, stop=True,
                            skip_group_check=True)
                    ins.then_inc(S["tO"], 1)

            if upto < 3:
                return
            t.wait_ge(S["gLoad"], 16 * 12)
            for i in range(NT):
                wge(t, "vZ1", i - 1)
                if i == 0:
                    t.wait_ge(S["vOev"], 64)
                wge(t, "vOev", 8 * (i // 4 + 1))
                base = 1024 + (i % 2) * 1024
                ins = None
                for c in range(8):
                    for nh in range(2):
                        ins = t.matmul(
                            psA[:, base + nh * 512: base + nh * 512 + 512],
                            actT[:, c, i * 128:(i + 1) * 128],
                            actU[:, c, 3072 + nh * 512: 3072 + nh * 512 + 512],
                            start=(c == 0), stop=False,
                            skip_group_check=True)
                for nh in range(2):
                    ins = t.matmul(
                        psA[:, base + nh * 512: base + nh * 512 + 512],
                        onesb[0:1, :],
                        bias_b[0:1, 3, nh * 512:nh * 512 + 512],
                        start=False, stop=True, skip_group_check=True)
                ins.then_inc(S["tZ"], 1)
                if lvl >= 4:
                    t.wait_ge(S["pXQ"], i + 1)
                    wge(t, "aXQe", i)
                    if i == 0:
                        t.wait_ge(S["vKV"], 1)
                        t.wait_ge(S["vSW"], NW)
                    ins = None
                    for c in range(8):
                        dst = bankbf3(6 + c // 4)
                        ins = t.transpose(dst[:, c % 4, :],
                                          xqbf[i % 2][:, c * 128:(c + 1) * 128],
                                          idn[:])
                    ins.then_inc(S["tXQT"], 1)

            if upto < 4:
                return
            GB = (P3A_GB * 4)[:4]
            UB = (P3A_UB * 4)[:4]
            t.wait_ge(S["vOev"], 64)
            t.wait_ge(S["vZ1"], NT)
            t.wait_ge(S["sX1"], 16 * NT)
            t.wait_ge(S["vKV"], 1)
            t.wait_ge(S["aXQe"], NT)
            for i in range(NT):
                t.wait_ge(S["dXQT"], 16 * (i + 1))
                xv = xqTi[i % 2].rearrange("p (a b) -> p a b", b=128)
                for os in range(8):
                    wge(t, "aGev", 8 * i + os - (len(P3A_GB) - 1))
                    if i == 0:
                        t.wait_ge(S["dWgT"], 128 * (os // 2 + 1))
                    ins = None
                    for c in range(8):
                        ins = t.matmul(bank(GB[os % len(P3A_GB)]), xv[:, c, :],
                                       actT[:, c, os * 512:(os + 1) * 512],
                                       start=(c == 0), stop=(c == 7),
                                       skip_group_check=True)
                    ins.then_inc(S["tG"], 1)
                    wge(t, "vUev", 8 * i + os - (len(P3A_UB) - 1))
                    if i == 0:
                        t.wait_ge(S["dWuT"], 128 * (os // 2 + 1))
                    ins = None
                    for c in range(8):
                        ins = t.matmul(bank(UB[os % len(P3A_UB)]), xv[:, c, :],
                                       actU[:, c, os * 512:(os + 1) * 512],
                                       start=(c == 0), stop=(c == 7),
                                       skip_group_check=True)
                    ins.then_inc(S["tU"], 1)

            if upto < 5:
                return
            t.wait_ge(S["vHI8"], NT)
            t.wait_ge(S["aGev"], 8 * NT)
            t.wait_ge(S["vUev"], 8 * NT)
            for i in range(NT):
                t.wait_ge(S["dHQ"], 512 * (i + 1))
                hv = hqT[i % 2].rearrange("p (a b) -> p a b", b=128)
                for nh in range(2):
                    mi = 2 * i + nh
                    wge(t, "vDev", mi - 3)
                    ins = None
                    for c in range(32):
                        if i == 0 and nh == 0:
                            t.wait_ge(S["dWdT"], 16 * (c + 1))
                        ins = t.matmul(bank(4 + mi % 4), hv[:, c, :],
                                       actTf[:, c * 1024 + nh * 512:
                                             c * 1024 + nh * 512 + 512],
                                       start=(c == 0), stop=(c == 31),
                                       skip_group_check=True)
                    ins.then_inc(S["tD"], 1)

        @block.vector
        def _(v):
            fen = [0]

            def vfence(ins):
                fen[0] += 1
                ins.then_inc(S["vFen"], 1)
                v.wait_ge(S["vFen"], fen[0])

            def vrecip(out, in_, pre_ins=None):
                if pre_ins is not None:
                    vfence(pre_ins)
                r = v.reciprocal(out, in_)
                vfence(r)
                return r
            for j in range(8):
                v.wait_ge(S["tBias"], j + 1)
                v.tensor_copy(bias_b[:, j // 2, (j % 2) * 512:(j % 2) * 512 + 512],
                              bank(j % 2)).then_inc(S["vBias"], 1)
            v.wait_ge(S["aSig"], 1)
            v.tensor_scalar_mul(bst[0:1, 1056:1056 + H],
                                bst[0:1, 1040:1040 + H],
                                LN4096).then_inc(S["vSig"], 1)
            v.wait_ge(S["aExp"], 1)
            vrecip(bst[0:1, 1088:1088 + H], bst[0:1, 1072:1072 + H])
            v.engine_nop().then_inc(S["vRin"], 1)
            v.wait_ge(S["tRsc"], 1)
            v.tensor_copy(rsc[:], psA[:, 3584:3584 + H]).then_inc(S["vRsc"], 1)

            def mean_chunk(j):
                v.wait_ge(S["dWm"], 16 * (j + 1))
                v.tensor_reduce(macc[:, j:j + 1].rearrange("p (a o) -> p a o", o=1),
                                wbuf[j % 2].rearrange("p (a b) -> p a b", a=1),
                                AX.X, ALU.add,
                                apply_absolute_value=True).then_inc(S["vMean"], 1)

            def mred_fin(w):
                v.tensor_reduce(mred[:, w:w + 1].rearrange("p (a o) -> p a o", o=1),
                                macc[:, w * NCH:(w + 1) * NCH]
                                .rearrange("p (a b) -> p a b", a=1),
                                AX.X, ALU.add).then_inc(S["vMred"], 1)

            def sw_fin(w):
                v.wait_ge(S["sMR"], 16 * (2 * w + 2))
                vfence(v.tensor_reduce(bst[0:1, 1104 + w:1105 + w]
                                       .rearrange("p (a o) -> p a o", o=1),
                                       bst[0:1, 1152:1280]
                                       .rearrange("p (a b) -> p a b", a=1),
                                       AX.X, ALU.add))
                vfence(v.tensor_scalar_mul(bst[0:1, 1108 + w:1109 + w],
                                           bst[0:1, 1104 + w:1105 + w], INV_N))
                mx = v.tensor_scalar_max(bst[0:1, 1112 + w:1113 + w],
                                         bst[0:1, 1108 + w:1109 + w], 1e-5)
                vrecip(bst[0:1, 1120 + 2 * w:1121 + 2 * w],
                       bst[0:1, 1112 + w:1113 + w], pre_ins=mx)
                vrecip(bst[0:1, 1121 + 2 * w:1122 + 2 * w],
                       bst[0:1, 1120 + 2 * w:1121 + 2 * w])
                v.engine_nop().then_inc(S["vSWr"], 1)
                v.wait_ge(S["tSWb"], w + 1)
                v.tensor_copy(swuw[:, 2 * w:2 * w + 2],
                              psA[:, 1540 + 2 * w:1542 + 2 * w]
                              ).then_inc(S["vSW"], 1)

            mean_emitted = 0
            fin_emitted = 0
            if upto == 0:
                for j in range(NQC):
                    mean_chunk(j)
                for w in range(NW):
                    mred_fin(w)
                    sw_fin(w)
                return
            for i in range(NT):
                v.wait_ge(S["tQ"], i + 1)
                v.tensor_tensor(qf[i % 2], psA[:, 0:1024], bias_b[:, 0, :],
                                ALU.add).then_inc(S["vQe"], 1)
                v.wait_ge(S["aSQq"], i + 1)
                v.tensor_reduce(rq[i % 2].rearrange("p (a o) -> p a o", o=1),
                                sqq.rearrange("p (a b) -> p a b", b=HD),
                                AX.X, ALU.add).then_inc(S["vRQ"], 1)
                v.wait_ge(S["tK"], i + 1)
                wge(v, "pKN", i - 1)
                v.tensor_tensor(kf[i % 2], psA[:, 1024:2048], bias_b[:, 1, :],
                                ALU.add).then_inc(S["vKe"], 1)
                v.wait_ge(S["aSQk"], i + 1)
                v.tensor_reduce(rk[i % 2].rearrange("p (a o) -> p a o", o=1),
                                sqk.rearrange("p (a b) -> p a b", b=HD),
                                AX.X, ALU.add).then_inc(S["vRK"], 1)
                v.wait_ge(S["tV"], i + 1)
                wge(v, "tKV", i - 1)
                v.tensor_tensor(vb[i % 2], psA[:, 2048:3072], bias_b[:, 2, :],
                                ALU.add).then_inc(S["vVB"], 1)
                v.wait_ge(S["aRTq"], i + 1)
                wge(v, "sQN", 16 * (i - 1))
                mx = v.tensor_scalar_max(nrm, srtq[i % 2], 1e-12)
                vrecip(rinvq, nrm, pre_ins=mx)
                a3, b3 = broadcast_tensor_aps(
                    qf[i % 2].rearrange("p (a b) -> p a b", b=HD),
                    rinvq.rearrange("p (a o) -> p a o", o=1))
                v.tensor_tensor(qn[i % 2].rearrange("p (a b) -> p a b", b=HD),
                                a3, b3, ALU.mult).then_inc(S["vQN"], 1)
                v.wait_ge(S["aRTk"], i + 1)
                wge(v, "tKV", i - 1)
                mx = v.tensor_scalar_max(nrm, srtk[i % 2], 1e-12)
                vrecip(rinvk[i % 2], nrm, pre_ins=mx)
                a3, b3 = broadcast_tensor_aps(
                    kf[i % 2].rearrange("p (a b) -> p a b", b=HD),
                    rinvk[i % 2].rearrange("p (a o) -> p a o", o=1))
                v.tensor_tensor(kn[i % 2].rearrange("p (a b) -> p a b", b=HD),
                                a3, b3, ALU.mult).then_inc(S["pKN"], 1)
                while mean_emitted < NQC and MEAN_TILE[mean_emitted] <= i:
                    mean_chunk(mean_emitted)
                    mean_emitted += 1
                while fin_emitted < NW and FIN_TILE[fin_emitted] <= i:
                    mred_fin(fin_emitted)
                    fin_emitted += 1
            v.wait_ge(S["tKV"], NT)
            ins = None
            for h in range(H):
                r0 = 64 * (h % 2)
                col = 3072 + 64 * h
                ins = v.tensor_scalar_mul(kvbf[r0:r0 + 64, h * HD:(h + 1) * HD],
                                          psA[r0:r0 + 64, col:col + 64],
                                          rsc[r0:r0 + 64, h:h + 1])
            ins.then_inc(S["vKV"], 1)
            for w in range(NW):
                sw_fin(w)
            if upto < 2:
                return
            for s in range(8):
                for c in range(8):
                    cs = 8 * s + c
                    v.wait_ge(S["tO"], cs + 1)
                    v.tensor_copy(actT[:, c, s * 512:(s + 1) * 512],
                                  bank(cs % 4)).then_inc(S["vOev"], 1)
            if upto < 3:
                return
            for i in range(NT):
                base = 1024 + (i % 2) * 1024
                v.wait_ge(S["tZ"], i + 1)
                v.wait_ge(S["dXin"], 16 * (i + 1))
                v.tensor_tensor(z2, psA[:, base:base + 1024], xres[i % 2],
                                ALU.add).then_inc(S["vZ1"], 1)
                if lvl < 2:
                    continue
                v.tensor_reduce(st2c(0).rearrange("p (a o) -> p a o", o=1),
                                z2.rearrange("p (a b) -> p a b", a=1),
                                AX.X, ALU.add)
                v.wait_ge(S["aSQ2"], i + 1)
                vfence(v.tensor_reduce(st2c(1).rearrange("p (a o) -> p a o", o=1),
                                       sq2.rearrange("p (a b) -> p a b", a=1),
                                       AX.X, ALU.add))
                v.tensor_scalar_mul(st2c(2), st2c(0), 1.0 / D)
                vfence(v.tensor_scalar_mul(st2c(3), st2c(1), 1.0 / D))
                vfence(v.tensor_tensor(st2c(4), st2c(2), st2c(2), ALU.mult))
                v.tensor_tensor(st2c(5), st2c(3), st2c(4),
                                ALU.subtract).then_inc(S["vVar2"], 1)
                v.wait_ge(S["aRS2"], i + 1)
                wge(v, "sX1", 16 * (i - 1))
                vrecip(st2c(6), st2c(7))
                v.tensor_scalar(x1b[i % 2], z2, st2c(2), st2c(6),
                                ALU.subtract, ALU.mult).then_inc(S["vX1"], 1)
                if lvl < 3:
                    continue
                vfence(v.tensor_reduce(st2c(8).rearrange("p (a o) -> p a o", o=1),
                                       x1b[i % 2].rearrange("p (a b) -> p a b", a=1),
                                       AX.X, ALU.max, apply_absolute_value=True))
                mx = v.tensor_scalar_max(st2c(9), st2c(8), 1e-5)
                vrecip(st2c(11), st2c(9), pre_ins=mx)
                ml = v.tensor_scalar_mul(st2c(10), st2c(11), 127.0)
                vrecip(sxs[:, i:i + 1], st2c(10), pre_ins=ml)
                vfence(v.tensor_scalar_mul(sq2, x1b[i % 2], st2c(10)))
                v.tensor_copy(xi8, sq2).then_inc(S["vXI8"], 1)
                if lvl < 4:
                    continue
                wge(v, "tXQT", i)
                v.tensor_copy(xqbf[i % 2], xi8).then_inc(S["pXQ"], 1)
                for j in P2B_QUANT.get(i, []):
                    v.wait_ge(S["aWM"], j + 1)
                    wge(v, "sWQ", 16 * j)
                    v.tensor_copy(i8q, wbuf[j % 2])
                    v.tensor_copy(wqbf[0], i8q)
                    v.tensor_scalar(wqbf[0], wqbf[0], -1.0, 1.0,
                                    ALU.max, ALU.min).then_inc(S["pQbf"], 1)
            if upto < 4:
                return
            v.wait_ge(S["sWQ"], 16 * NQC)
            for i in range(NT):
                wge(v, "aGev", 8 * i)
                v.tensor_tensor(sgu[:, 0:1], sxs[:, i:i + 1], swuw[:, 1:2],
                                ALU.mult)
                vfence(v.tensor_tensor(sgu[:, 1:2], sxs[:, i:i + 1],
                                       swuw[:, 3:4], ALU.mult))
                v.engine_nop().then_inc(S["vSG"], 1)
                for os in range(8):
                    v.wait_ge(S["tU"], 8 * i + os + 1)
                    v.wait_ge(S["aGev"], 8 * i + os + 1)
                    ins = v.scalar_tensor_tensor(hbuf[:, os * 512:(os + 1) * 512],
                                                 bank((P3A_UB * 4)[os % len(P3A_UB)]),
                                                 sgu[:, 1:2],
                                                 gbuf[:, os * 512:(os + 1) * 512],
                                                 ALU.mult, ALU.mult)
                    if os == 7:
                        vfence(ins)
                        v.engine_nop().then_inc(S["vUev"], 1)
                    else:
                        ins.then_inc(S["vUev"], 1)
                vfence(v.tensor_reduce(st3c(0).rearrange("p (a o) -> p a o", o=1),
                                       hbuf.rearrange("p (a b) -> p a b", a=1),
                                       AX.X, ALU.max, apply_absolute_value=True))
                mx = v.tensor_scalar_max(st3c(1), st3c(0), 1e-5)
                vrecip(st3c(3), st3c(1), pre_ins=mx)
                ml = v.tensor_scalar_mul(st3c(2), st3c(3), 127.0)
                vrecip(sds[:, i:i + 1], st3c(2), pre_ins=ml)
                v.engine_nop().then_inc(S["vSH"], 1)
                v.wait_ge(S["aHM"], i + 1)
                v.tensor_copy(hi8, gbuf).then_inc(S["vHI8"], 1)
                wge(v, "sHQ", 16 * (i - 1))
                v.tensor_copy(hqb[i % 2], hi8).then_inc(S["pHQ"], 1)
                for j in P3A_QUANT.get(i, []):
                    v.wait_ge(S["aWM"], j + 1)
                    wge(v, "sWQ", 16 * j)
                    v.tensor_copy(i8q, wbuf[j % 2])
                    v.tensor_copy(wqbf[0], i8q)
                    v.tensor_scalar(wqbf[0], wqbf[0], -1.0, 1.0,
                                    ALU.max, ALU.min).then_inc(S["pQbf"], 1)
            if upto < 5:
                return
            for i in range(NT):
                vfence(v.tensor_tensor(sgu[:, 2:3], sds[:, i:i + 1],
                                       swuw[:, 5:6], ALU.mult))
                v.wait_ge(S["dX1"], 16 * (i + 1))
                for nh in range(2):
                    mi = 2 * i + nh
                    v.wait_ge(S["tD"], mi + 1)
                    v.scalar_tensor_tensor(z3[:, nh * 512:(nh + 1) * 512],
                                           bank(4 + mi % 4), sgu[:, 2:3],
                                           x1i[i % 2][:, nh * 512:(nh + 1) * 512],
                                           ALU.mult, ALU.add
                                           ).then_inc(S["vDev"], 1)
                v.tensor_reduce(st4c(0).rearrange("p (a o) -> p a o", o=1),
                                z3.rearrange("p (a b) -> p a b", a=1),
                                AX.X, ALU.add)
                v.wait_ge(S["aSQ3"], i + 1)
                vfence(v.tensor_reduce(st4c(1).rearrange("p (a o) -> p a o", o=1),
                                       sq3.rearrange("p (a b) -> p a b", a=1),
                                       AX.X, ALU.add))
                v.tensor_scalar_mul(st4c(2), st4c(0), 1.0 / D)
                vfence(v.tensor_scalar_mul(st4c(3), st4c(1), 1.0 / D))
                vfence(v.tensor_tensor(st4c(4), st4c(2), st4c(2), ALU.mult))
                v.tensor_tensor(st4c(5), st4c(3), st4c(4),
                                ALU.subtract).then_inc(S["vVar3"], 1)
                v.wait_ge(S["aRS3"], i + 1)
                wge(v, "sOUT", 16 * (i - 1))
                vrecip(st4c(6), st4c(7))
                v.tensor_scalar(outb[i % 2], z3, st4c(2), st4c(6),
                                ALU.subtract, ALU.mult).then_inc(S["vOut"], 1)

        @block.scalar
        def _(s):
            s.wait_ge(S["dBias"], 16 * 5)
            s.activation(bst[0:1, 1040:1040 + H], bst[0:1, 1024:1024 + H],
                         AF.Sigmoid).then_inc(S["aSig"], 1)
            s.wait_ge(S["vSig"], 1)
            s.activation(bst[0:1, 1072:1072 + H], bst[0:1, 1056:1056 + H],
                         AF.Exp).then_inc(S["aExp"], 1)

            qwj = [0]

            def qwrites(upto):
                while qwj[0] < upto:
                    j = qwj[0]
                    s.wait_ge(S["pQbf"], j + 1)
                    dst, src = wsb_out(j)
                    s.dma_start(out=dst, in_=src).then_inc(S["sWQ"], 16)
                    qwj[0] += 1

            def mr_bounce():
                for w in range(NW):
                    s.wait_ge(S["vMred"], w + 1)
                    wge(s, "vSWr", w)
                    s.dma_start(out=mrsc[w, :],
                                in_=mred[:, w:w + 1]).then_inc(S["sMR"], 16)
                    s.wait_ge(S["sMR"], 16 * (2 * w + 1))
                    s.dma_start(out=bst[0:1, 1152:1280],
                                in_=mrsc[w:w + 1, :]).then_inc(S["sMR"], 16)

            if upto == 0:
                mr_bounce()
                return
            for i in range(NT):
                s.wait_ge(S["vQe"], i + 1)
                wge(s, "vRQ", i)
                s.activation(sqq, qf[i % 2], AF.Square).then_inc(S["aSQq"], 1)
                s.wait_ge(S["vRQ"], i + 1)
                wge(s, "vQN", i - 1)
                s.activation(srtq[i % 2], rq[i % 2], AF.Sqrt).then_inc(S["aRTq"], 1)
                s.wait_ge(S["vKe"], i + 1)
                wge(s, "vRK", i)
                s.activation(sqk, kf[i % 2], AF.Square).then_inc(S["aSQk"], 1)
                s.wait_ge(S["vRK"], i + 1)
                wge(s, "pKN", i - 1)
                s.activation(srtk[i % 2], rk[i % 2], AF.Sqrt).then_inc(S["aRTk"], 1)
                s.wait_ge(S["vQN"], i + 1)
                s.dma_start(out=qsc[i * 128:(i + 1) * 128, :],
                            in_=qn[i % 2]).then_inc(S["sQN"], 16)
                qwrites(sum(len(P1_QUANT.get(k, [])) for k in range(i + 1)))
            mr_bounce()
            if upto < 3:
                s.wait_ge(S["sQN"], 16 * NT)
                return
            qwrites(N_P1Q + len(P2A_QUANT))
            if dbg and upto >= 2:
                s.wait_ge(S["vKV"], 1)
                s.dma_start(out=dbg_kv, in_=kvbf[:]).then_inc(S["sDBG"], 16)
                s.wait_ge(S["vOev"], 64)
                s.dma_start(out=dbg_ofm, in_=actT[:]).then_inc(S["sDBG"], 16)
                s.wait_ge(S["vSW"], NW)
                s.dma_start(out=dbg_macc, in_=arena[:, 9408:9459]
                            ).then_inc(S["sDBG"], 16)
                s.dma_start(out=dbg_swuw, in_=swuw[:]).then_inc(S["sDBG"], 16)
                s.dma_start(out=dbg_bst, in_=bst[0:1, 1088:1152]
                            ).then_inc(S["sDBG"], 16)
            for i in range(NT):
                if lvl >= 2:
                    s.wait_ge(S["vZ1"], i + 1)
                    wge(s, "vXI8" if lvl >= 3 else "vVar2", i)
                    s.activation(sq2, z2, AF.Square).then_inc(S["aSQ2"], 1)
                    s.wait_ge(S["vVar2"], i + 1)
                    s.activation(st2c(7), st2c(5), AF.Sqrt,
                                 bias=cst[:, 1:2]).then_inc(S["aRS2"], 1)
                    s.wait_ge(S["vX1"], i + 1)
                    s.dma_start(out=x1sc[i * 128:(i + 1) * 128, :],
                                in_=x1b[i % 2]).then_inc(S["sX1"], 16)
                if lvl >= 4:
                    s.wait_ge(S["tXQT"], i + 1)
                    wge(s, "sXQT", 16 * (i - 1))
                    xv = xqTt[i % 2].rearrange("p (a b) -> p a b", b=128)
                    s.copy(xv[:, 0:4, :], bankbf3(6)[:, 0:4, :])
                    s.copy(xv[:, 4:8, :],
                           bankbf3(7)[:, 0:4, :]).then_inc(S["aXQe"], 1)
                    s.dma_start(out=xqT_sc.transpose([1, 0, 2])[:, :, i * 128:(i + 1) * 128],
                                in_=xqTt[i % 2].rearrange("p (a b) -> p a b", b=128)
                                ).then_inc(S["sXQT"], 16)
                for j in P2B_QUANT.get(i, []):
                    w = j // NCH
                    s.wait_ge(S["dWq"], 16 * (j + 1))
                    s.wait_ge(S["vSW"], w + 1)
                    wge(s, "pQbf", j - 1)
                    s.activation(wbuf[j % 2], wbuf[j % 2], AF.Copy,
                                 scale=swuw[:, 2 * w:2 * w + 1]
                                 ).then_inc(S["aWM"], 1)
                qwrites(N_P1Q + len(P2A_QUANT)
                        + sum(len(P2B_QUANT.get(k, [])) for k in range(i + 1)))
            qwrites(NQC)
            if upto < 4:
                if lvl >= 2:
                    s.wait_ge(S["sX1"], 16 * NT)
                if lvl >= 4:
                    s.wait_ge(S["sXQT"], 16 * NT)
                s.wait_ge(S["sWQ"], 16 * NQC)
                return
            for i in range(NT):
                for os in range(8):
                    s.wait_ge(S["tG"], 8 * i + os + 1)
                    s.wait_ge(S["vSG"], i + 1)
                    if os == 0:
                        wge(s, "vHI8", i)
                    wge(s, "vUev", 8 * (i - 1) + os + 1)
                    s.activation(gbuf[:, os * 512:(os + 1) * 512],
                                 bank((P3A_GB * 4)[os % len(P3A_GB)]), AF.Silu,
                                 scale=sgu[:, 0:1]).then_inc(S["aGev"], 1)
                s.wait_ge(S["vSH"], i + 1)
                s.wait_ge(S["vUev"], 8 * (i + 1))
                s.activation(gbuf, hbuf, AF.Copy,
                             scale=st3c(2)).then_inc(S["aHM"], 1)
                for j in P3A_QUANT.get(i, []):
                    w = j // NCH
                    s.wait_ge(S["dWq"], 16 * (j + 1))
                    s.wait_ge(S["vSW"], w + 1)
                    wge(s, "pQbf", j - 1)
                    s.activation(wbuf[j % 2], wbuf[j % 2], AF.Copy,
                                 scale=swuw[:, 2 * w:2 * w + 1]
                                 ).then_inc(S["aWM"], 1)
                s.wait_ge(S["pHQ"], i + 1)
                s.dma_start(out=hqsc[i * 128:(i + 1) * 128, :],
                            in_=hqb[i % 2]).then_inc(S["sHQ"], 16)
                qwrites(32 + sum(len(P3A_QUANT.get(k, [])) for k in range(i + 1)))
            if upto < 5:
                s.wait_ge(S["sHQ"], 16 * NT)
                return
            qwrites(NQC)
            def x1_load(i):
                if i == 0:
                    s.wait_ge(S["vHI8"], NT)
                s.wait_ge(S["sX1"], 16 * (i + 1))
                wge(s, "vDev", 2 * (i - 1))
                s.dma_start(out=x1i[i % 2],
                            in_=x1sc[i * 128:(i + 1) * 128, :]
                            ).then_inc(S["dX1"], 16)

            x1_load(0)
            x1_load(1)
            for i in range(NT):
                s.wait_ge(S["vDev"], 2 * i + 2)
                wge(s, "vVar3", i)
                s.activation(sq3, z3, AF.Square).then_inc(S["aSQ3"], 1)
                if i + 2 < NT:
                    x1_load(i + 2)
                s.wait_ge(S["vVar3"], i + 1)
                s.activation(st4c(7), st4c(5), AF.Sqrt,
                             bias=cst[:, 1:2]).then_inc(S["aRS3"], 1)
                s.wait_ge(S["vOut"], i + 1)
                s.dma_start(out=out_h[i * 128:(i + 1) * 128, :],
                            in_=outb[i % 2]).then_inc(S["sOUT"], 16)
            s.wait_ge(S["sOUT"], 16 * NT)

        stack.close()
    return nc


_NC_CACHE = []


def kernel(L0_attn_x, L0_attn_Wq, L0_attn_bq, L0_attn_Wk, L0_attn_bk,
           L0_attn_Wv, L0_attn_bv, L0_attn_Wo, L0_attn_bo, L0_attn_m,
           L0_ffn_w_gate, L0_ffn_w_up, L0_ffn_w_down):
    if not _NC_CACHE:
        _NC_CACHE.append(build())
    nc = _NC_CACHE[0]

    def f(a):
        return np.ascontiguousarray(np.asarray(a, dtype=np.float32))

    X = f(L0_attn_x)
    shared = {
        "wq": f(L0_attn_Wq), "wk": f(L0_attn_Wk), "wv": f(L0_attn_Wv),
        "wo": f(L0_attn_Wo), "bq": f(L0_attn_bq), "bk": f(L0_attn_bk),
        "bv": f(L0_attn_bv), "bo": f(L0_attn_bo), "m": f(L0_attn_m),
        "wg": f(L0_ffn_w_gate), "wu": f(L0_ffn_w_up), "wd": f(L0_ffn_w_down),
    }
    in_maps = [dict(shared, x=X[i]) for i in range(NCHIP)]
    res = run_bass_kernel_spmd(nc, in_maps, core_ids=list(range(NCHIP)))
    return np.stack([np.asarray(res.results[i]["out"], dtype=np.float32)
                     for i in range(NCHIP)], axis=0)


# version 08
# speedup vs baseline: 1.1972x; 1.1972x over previous
import math
import contextlib
import numpy as np
import concourse.bass as bass
import concourse.mybir as mybir
from concourse.bass import broadcast_tensor_aps
from concourse.bass_utils import run_bass_kernel_spmd

F32 = mybir.dt.float32
BF16 = mybir.dt.bfloat16
I8 = mybir.dt.int8
AF = mybir.ActivationFunctionType
ALU = mybir.AluOpType
AX = mybir.AxisListType

T = 4096
NT = 32
D = 1024
H = 16
HD = 64
NI = 4096
NCHIP = 8
INV_N = 1.0 / 4194304.0
LN4096 = math.log(4096.0)

NCH = 16
NW = 3
NQC = NW * NCH

MEAN_TILE = [min(31, int(j / 1.55)) for j in range(NQC)]
FIN_TILE = [MEAN_TILE[NCH - 1] + 1, MEAN_TILE[2 * NCH - 1] + 1, 31]
QSLOT = {j: 33 + (2 * j) // 3 for j in range(NQC)}
P1_QUANT = {i: [] for i in range(32)}
P2A_QUANT = []
P2B_QUANT = {i: [j for j, s in QSLOT.items() if s == 33 + i] for i in range(32)}
P3A_QUANT = {i: [] for i in range(32)}
N_P1Q = 0


P2B_LVL = 4
P3A_DEPTH = 4
P3A_GB = [0, 1, 4, 5]
P3A_UB = [2, 3, 6, 7]
P3B_SHIFT = False


def build(upto=5, dbg=False):
    nc = bass.Bass("TRN2", target_bir_lowering=False)
    lvl = P2B_LVL if upto == 3 else 4
    scratch_kind = "ExternalOutput" if dbg else "Internal"

    x_h = nc.dram_tensor("x", [T, D], F32, kind="ExternalInput").ap()
    wq_h = nc.dram_tensor("wq", [D, D], F32, kind="ExternalInput").ap()
    wk_h = nc.dram_tensor("wk", [D, D], F32, kind="ExternalInput").ap()
    wv_h = nc.dram_tensor("wv", [D, D], F32, kind="ExternalInput").ap()
    wo_h = nc.dram_tensor("wo", [D, D], F32, kind="ExternalInput").ap()
    bq_h = nc.dram_tensor("bq", [D], F32, kind="ExternalInput").ap()
    bk_h = nc.dram_tensor("bk", [D], F32, kind="ExternalInput").ap()
    bv_h = nc.dram_tensor("bv", [D], F32, kind="ExternalInput").ap()
    bo_h = nc.dram_tensor("bo", [D], F32, kind="ExternalInput").ap()
    m_h = nc.dram_tensor("m", [H], F32, kind="ExternalInput").ap()
    wg_h = nc.dram_tensor("wg", [NI, D], F32, kind="ExternalInput").ap()
    wu_h = nc.dram_tensor("wu", [NI, D], F32, kind="ExternalInput").ap()
    wd_h = nc.dram_tensor("wd", [D, NI], F32, kind="ExternalInput").ap()
    out_h = nc.dram_tensor("out", [T, D], F32, kind="ExternalOutput").ap()

    xbf = nc.dram_tensor("xbf", [T, D], BF16).ap()
    xqsc = nc.dram_tensor("xqsc", [T, D], BF16).ap()
    qsc = nc.dram_tensor("qsc", [T, D], BF16, kind=scratch_kind).ap()
    xqT_sc = nc.dram_tensor("xqT_sc", [8, 128, T], BF16, kind=scratch_kind).ap()
    x1sc = nc.dram_tensor("x1sc", [T, D], F32, kind=scratch_kind).ap()
    hqsc = nc.dram_tensor("hqsc", [T, NI], BF16, kind=scratch_kind).ap()
    wgq = nc.dram_tensor("wgq", [NI, D], BF16, kind=scratch_kind).ap()
    wuq = nc.dram_tensor("wuq", [NI, D], BF16).ap()
    wdq = nc.dram_tensor("wdq", [D, NI], BF16).ap()
    mrsc = nc.dram_tensor("mrsc", [NW, 128], F32).ap()
    if dbg:
        dbg_kv = nc.dram_tensor("dbg_kv", [128, D], BF16,
                                kind="ExternalOutput").ap()
        dbg_ofm = nc.dram_tensor("dbg_ofm", [128, 8, T], BF16,
                                 kind="ExternalOutput").ap()
        dbg_macc = nc.dram_tensor("dbg_macc", [128, 51], F32,
                                  kind="ExternalOutput").ap()
        dbg_swuw = nc.dram_tensor("dbg_swuw", [128, 8], F32,
                                  kind="ExternalOutput").ap()
        dbg_bst = nc.dram_tensor("dbg_bst", [1, 64], F32,
                                 kind="ExternalOutput").ap()
    wq_src = [wg_h, wu_h, wd_h]
    wq_dst = [wgq, wuq, wdq]

    def wchunk(ap, j):
        r, c = ap.shape
        v = ap.rearrange("(a k) n -> k a n", k=128)
        if c >= 2048:
            n2 = c // 2048
            return v[:, j // n2, (j % n2) * 2048:(j % n2) * 2048 + 2048], "2d"
        na = 2048 // c
        return v[:, j * na:(j + 1) * na, :], "3d"

    with (
        nc.sbuf_tensor("actT", [128, 8, T], BF16) as actT,
        nc.sbuf_tensor("actU", [128, 8, T], BF16) as actU,
        nc.sbuf_tensor("bias_b", [128, 4, D], BF16) as bias_b,
        nc.sbuf_tensor("kvbf", [128, D], BF16) as kvbf,
        nc.sbuf_tensor("rsc", [128, H], F32) as rsc,
        nc.sbuf_tensor("sxs", [128, NT], F32) as sxs,
        nc.sbuf_tensor("sds", [128, NT], F32) as sds,
        nc.sbuf_tensor("swuw", [128, 8], F32) as swuw,
        nc.sbuf_tensor("cst", [128, 4], F32) as cst,
        nc.sbuf_tensor("ones", [1, 128], F32) as ones,
        nc.sbuf_tensor("onesb", [1, 128], BF16) as onesb,
        nc.sbuf_tensor("bst", [1, 1280], F32) as bst,
        nc.sbuf_tensor("idn", [128, 128], BF16) as idn,
        nc.sbuf_tensor("arena", [128, 16160], F32) as arena,
        nc.psum_tensor("psA", [128, 4096], F32) as psA,
        nc.Block() as block,
    ):
        def af32(off, n):
            assert off + n <= 16160, (off, n)
            return arena[:, off:off + n]

        def abf(off, nf32):
            return af32(off, nf32).bitcast(BF16)

        def ai8(off, nf32):
            return af32(off, nf32).bitcast(I8)

        qf = [af32(0, 1024), af32(1024, 1024)]
        kf = [af32(2048, 1024), af32(3072, 1024)]
        sqq = af32(4096, 1024)
        sqk = af32(5120, 1024)
        qn = [abf(6144, 512), abf(6656, 512)]
        kn = [abf(7168, 512), abf(7680, 512)]
        vb = [abf(8192, 512), abf(8704, 512)]
        rq = [af32(9216, 16), af32(9232, 16)]
        rk = [af32(9248, 16), af32(9264, 16)]
        srtq = [af32(9280, 16), af32(9296, 16)]
        srtk = [af32(9312, 16), af32(9328, 16)]
        nrm = af32(9344, 16)
        rinvq = af32(9360, 16)
        rinvk = [af32(9376, 16), af32(9392, 16)]
        macc = af32(9408, 48)
        mred = af32(9456, 3)
        wbuf = [af32(9472, 2048), af32(11520, 2048)]
        i8q = ai8(13568, 512)
        wqbf = [abf(14080, 1024)]

        def wsb(j):
            ap, kind = wchunk(wq_src[j // NCH], j % NCH)
            if kind == "2d":
                return ap, wbuf[j % 2]
            return ap, wbuf[j % 2].rearrange("p (a b) -> p a b", b=1024)

        def wsb_out(j):
            ap, kind = wchunk(wq_dst[j // NCH], j % NCH)
            if kind == "2d":
                return ap, wqbf[0]
            return ap, wqbf[0].rearrange("p (a b) -> p a b", b=1024)

        xres = [af32(0, 1024), af32(1024, 1024)]
        z2 = af32(2048, 1024)
        sq2 = af32(3072, 1024)
        x1b = [af32(4096, 1024), af32(5120, 1024)]
        xqbf = [abf(6144, 512), abf(6656, 512)]
        xi8 = ai8(7168, 256)
        st2 = af32(8448, 32)
        xqTt = [abf(7424, 512), abf(7936, 512)]

        def st2c(k):
            return st2[:, k:k + 1]

        gbuf = af32(1024, 4096)
        hbuf = af32(5120, 4096)
        hqb = [abf(9216, 2048), abf(11264, 2048)]
        hi8 = ai8(13312, 1024)
        st3 = af32(14336, 32)
        sgu = af32(14368, 4)
        xqTi = [abf(15104, 512), abf(15616, 512)]

        def st3c(k):
            return st3[:, k:k + 1]

        hqi = [abf(0, 2048), abf(2048, 2048)]
        hqT = [abf(4096, 2048), abf(6144, 2048)]
        x1i = [af32(8192, 1024), af32(9216, 1024)]
        z3 = af32(10240, 1024)
        sq3 = af32(11264, 1024)
        outb = [af32(12288, 1024), af32(13312, 1024)]
        st4 = af32(16128, 32)

        def st4c(k):
            return st4[:, k:k + 1]

        actTf = actT[:].rearrange("p a b -> p (a b)")

        def bank(b):
            return psA[:, b * 512:(b + 1) * 512]

        def bankbf3(b):
            return psA[:, b * 512:(b + 1) * 512].bitcast(BF16).rearrange(
                "p (a b) -> p a b", b=128)

        semnames = [
            "dBias", "dWm", "dXT", "dWq", "dQT", "dXin", "dWgT", "dWuT",
            "dXQT", "dWdT", "dHQ", "dX1", "gLoad", "sMR",
            "sQN", "sWQ", "sX1", "sXQT", "sHQ", "sOUT", "sXQ", "dXB", "sXB",
            "tBias", "tRsc", "tSum", "tSWb", "tQ", "tK", "tV", "tKV",
            "tO", "tZ", "tXQT", "tG", "tU", "tTr", "tD",
            "vBias", "vSig", "vRin", "vRsc", "vMean", "vMred", "vSWr",
            "vSW", "vQe", "vRQ", "vKe", "vRK", "vVB", "vQN", "vRKi",
            "vKV", "vOev", "vZ1", "vVar2", "vX1", "vXI8", "vXQTe",
            "vSG", "vUev", "vSH", "vHI8", "vHTe", "vDev", "vVar3", "vOut",
            "aSig", "aExp", "aSQq", "aRTq", "aSQk", "aRTk", "aSQ2",
            "aRS2", "aGev", "aHTe", "aSQ3", "aRS3", "aWM", "aHM", "aXQe",
            "pConst", "pKN", "pZ", "pXQ", "pQbf", "pHm", "pHQ", "sDBG",
            "vFen",
        ]
        stack = contextlib.ExitStack()
        S = {n: stack.enter_context(nc.semaphore(n)) for n in semnames}

        def wge(eng, name, v):
            if v > 0:
                eng.wait_ge(S[name], v)

        @block.sync
        def _(sync):
            for j, bh in enumerate([bq_h, bk_h, bv_h, bo_h]):
                wge(sync, "tBias", 2 * j)
                sync.dma_start(out=bst[0:1, 0:D],
                               in_=bh.rearrange("(o n) -> o n", o=1)
                               ).then_inc(S["dBias"], 16)
            sync.dma_start(out=bst[0:1, 1024:1024 + H],
                           in_=m_h.rearrange("(o n) -> o n", o=1)
                           ).then_inc(S["dBias"], 16)
            for c in range(8):
                sync.wait_ge(S["gLoad"], 16 * (c + 1))
                sync.dma_start(out=actT[:, c, :],
                               in_=xbf[:, c * 128:(c + 1) * 128],
                               transpose=True).then_inc(S["dXT"], 16)

            def qT_batch(qt):
                wge(sync, "tV", 8 * (qt + 1) if qt < 3 else NT)
                sync.wait_ge(S["sQN"], 16 * 8 * (qt + 1))
                for c in range(8):
                    sync.dma_start(out=actT[:, c, qt * 1024:(qt + 1) * 1024],
                                   in_=qsc[qt * 1024:(qt + 1) * 1024,
                                           c * 128:(c + 1) * 128],
                                   transpose=True).then_inc(S["dQT"], 16)

            QT_AT = {15: 0, 27: 1, 40: 2}
            for j in range(NQC):
                wge(sync, "vMean", j - 1)
                src, dst = wsb(j)
                sync.dma_start(out=dst, in_=src).then_inc(S["dWm"], 16)
                if upto >= 2 and j in QT_AT:
                    qT_batch(QT_AT[j])
            if upto < 2:
                return
            qT_batch(3)
            if upto < 3:
                return

            def wgT_batch(qt):
                wge(sync, "tZ", 8 * (qt + 1) if qt < 3 else NT)
                if dbg:
                    sync.wait_ge(S["sDBG"], 32)
                sync.wait_ge(S["sWQ"], 16 * (4 * qt + 4))
                for c in range(8):
                    sync.dma_start(out=actT[:, c, qt * 1024:(qt + 1) * 1024],
                                   in_=wgq[qt * 1024:(qt + 1) * 1024,
                                           c * 128:(c + 1) * 128],
                                   transpose=True).then_inc(S["dWgT"], 16)

            def wuT_batch(qt):
                if qt == 3:
                    sync.wait_ge(S["tZ"], NT)
                sync.wait_ge(S["sWQ"], 16 * (16 + 4 * qt + 4))
                for c in range(8):
                    sync.dma_start(out=actU[:, c, qt * 1024:(qt + 1) * 1024],
                                   in_=wuq[qt * 1024:(qt + 1) * 1024,
                                           c * 128:(c + 1) * 128],
                                   transpose=True).then_inc(S["dWuT"], 16)

            def xqTi_load(i):
                wge(sync, "sXQT", 16 * (i + 1))
                wge(sync, "tU", 8 * (i - 1))
                sync.dma_start(out=xqTi[i % 2].rearrange("p (a b) -> p a b", b=128),
                               in_=xqT_sc.transpose([1, 0, 2])[:, :, i * 128:(i + 1) * 128]
                               ).then_inc(S["dXQT"], 16)

            for i in range(NT):
                wge(sync, "vZ1", i - 1)
                sync.dma_start(out=xres[i % 2],
                               in_=x_h[i * 128:(i + 1) * 128, :]
                               ).then_inc(S["dXin"], 16)
                if upto >= 4:
                    if i == 9:
                        wgT_batch(0)
                    elif i == 10:
                        xqTi_load(0)
                    elif i == 12:
                        xqTi_load(1)
                    elif i == 13:
                        wuT_batch(0)
                    elif i == 17:
                        wgT_batch(1)
                    elif i == 19:
                        wuT_batch(1)
                    elif i == 25:
                        wgT_batch(2)
                    elif i == 27:
                        wuT_batch(2)
            if upto < 4:
                return
            wgT_batch(3)
            wuT_batch(3)
            for i in range(2, NT):
                xqTi_load(i)
            if upto < 5:
                return
            def hqi_load(i):
                if i == 0:
                    sync.wait_ge(S["vHI8"], NT)
                sync.wait_ge(S["sHQ"], 16 * (i + 1))
                wge(sync, "tTr", 8 * (i - 1))
                sync.dma_start(out=hqi[i % 2],
                               in_=hqsc[i * 128:(i + 1) * 128, :]
                               ).then_inc(S["dHQ"], 16)

            sync.wait_ge(S["tU"], 8 * NT)
            for c in range(32):
                sync.dma_start(out=actTf[:, c * 1024:(c + 1) * 1024],
                               in_=wdq[:, c * 128:(c + 1) * 128],
                               transpose=True).then_inc(S["dWdT"], 16)
                if c == 7:
                    hqi_load(0)
                elif c == 15:
                    hqi_load(1)
            for i in range(2, NT):
                hqi_load(i)

        @block.gpsimd
        def _(g):
            for c in range(8):
                g.dma_start(out=xbf[:, c * 128:(c + 1) * 128],
                            in_=x_h[:, c * 128:(c + 1) * 128]
                            ).then_inc(S["gLoad"], 16)
            for j, wh in enumerate([wq_h, wk_h, wv_h, wo_h]):
                g.dma_start(out=actU[:, :, j * 1024:(j + 1) * 1024],
                            in_=wh.rearrange("(c k) n -> k c n", k=128)
                            ).then_inc(S["gLoad"], 16)
            g.memset(ones[:], 1.0)
            g.memset(onesb[:], 1.0)
            g.memset(cst[:, 0:1], 127.0)
            g.memset(cst[:, 1:2], 1e-5)
            g.memset(cst[:, 2:3], 1e-12)
            g.memset(cst[:, 3:4], 1.0)
            g.memset(idn[:], 0.0)
            g.affine_select(out=idn[:], in_=idn[:], compare_op=ALU.not_equal,
                            fill=1.0, base=0, pattern=[[-1, 128]],
                            channel_multiplier=1).then_inc(S["pConst"], 1)

            if upto < 3:
                return
            for j in range(NQC):
                if j < 2:
                    g.wait_ge(S["vMean"], NQC)
                else:
                    wge(g, "pQbf", j - 1)
                src, dst = wsb(j)
                g.dma_start(out=dst, in_=src).then_inc(S["dWq"], 16)

        @block.tensor
        def _(t):
            t.wait_ge(S["pConst"], 1)
            for j in range(8):
                t.wait_ge(S["dBias"], 16 * (j // 2 + 1))
                wge(t, "vBias", j - 1)
                t.matmul(bank(j % 2), ones[0:1, :],
                         bst[0:1, (j % 2) * 512:(j % 2) * 512 + 512],
                         start=True, stop=True,
                         skip_group_check=True).then_inc(S["tBias"], 1)
            t.wait_ge(S["vRin"], 1)
            t.matmul(psA[:, 3584:3584 + H], ones[0:1, :],
                     bst[0:1, 1088:1088 + H], start=True, stop=True,
                     skip_group_check=True).then_inc(S["tRsc"], 1)

            t.wait_ge(S["vBias"], 8)
            t.wait_ge(S["vRsc"], 1)
            if upto == 0:
                for w in range(NW):
                    wge(t, "vSW", w)
                    wge(t, "vSWr", w + 1)
                    t.matmul(psA[:, 1540 + 2 * w:1542 + 2 * w], ones[0:1, :],
                             bst[0:1, 1120 + 2 * w:1122 + 2 * w],
                             start=True, stop=True,
                             skip_group_check=True).then_inc(S["tSWb"], 1)
                return
            def kv_mms(i):
                t.wait_ge(S["pKN"], i + 1)
                t.wait_ge(S["vVB"], i + 1)
                ins = None
                for h in range(H):
                    ins = t.matmul(
                        psA[:, 3072 + 64 * h:3072 + 64 * h + 64],
                        kn[i % 2][:, (h // 2) * 128:(h // 2) * 128 + 128],
                        vb[i % 2][:, h * HD:(h + 1) * HD],
                        start=(i == 0 and h in (0, 8)),
                        stop=(i == NT - 1 and h == H - 1),
                        skip_group_check=True)
                ins.then_inc(S["tKV"], 1)

            for i in range(NT):
                for mi, (base, gate, sem) in enumerate(
                        [(0, "vQe", "tQ"), (1024, "vKe", "tK"),
                         (2048, "vVB", "tV")]):
                    wge(t, gate, i)
                    if i == 0:
                        t.wait_ge(S["gLoad"], 16 * (9 + mi))
                    ins = None
                    for c in range(8):
                        if i == 0 and mi == 0:
                            t.wait_ge(S["dXT"], 16 * (c + 1))
                        for nh in range(2):
                            ins = t.matmul(
                                psA[:, base + nh * 512: base + nh * 512 + 512],
                                actT[:, c, i * 128:(i + 1) * 128],
                                actU[:, c, mi * 1024 + nh * 512:
                                     mi * 1024 + nh * 512 + 512],
                                start=(c == 0), stop=(c == 7),
                                skip_group_check=True)
                    ins.then_inc(S[sem], 1)
                if i >= 1:
                    kv_mms(i - 1)
            kv_mms(NT - 1)
            t.wait_ge(S["vKe"], NT)
            for w in range(NW):
                wge(t, "vSW", w)
                wge(t, "vSWr", w + 1)
                t.matmul(psA[:, 1540 + 2 * w:1542 + 2 * w], ones[0:1, :],
                         bst[0:1, 1120 + 2 * w:1122 + 2 * w],
                         start=True, stop=True,
                         skip_group_check=True).then_inc(S["tSWb"], 1)

            if upto < 2:
                return
            t.wait_ge(S["vKV"], 1)
            t.wait_ge(S["vQe"], NT)
            t.wait_ge(S["vSW"], NW)
            for s in range(8):
                t.wait_ge(S["dQT"], 128 * (s // 2 + 1))
                for c in range(8):
                    cs = 8 * s + c
                    wge(t, "vOev", cs - 3)
                    ins = None
                    for h in (2 * c, 2 * c + 1):
                        r0 = 64 * (h % 2)
                        ins = t.matmul(
                            psA[r0:r0 + 64, (cs % 4) * 512:(cs % 4) * 512 + 512],
                            kvbf[r0:r0 + 64, h * HD:(h + 1) * HD],
                            actT[r0:r0 + 64, c, s * 512:(s + 1) * 512],
                            start=True, stop=True,
                            skip_group_check=True)
                    ins.then_inc(S["tO"], 1)

            if upto < 3:
                return
            t.wait_ge(S["gLoad"], 16 * 12)
            for i in range(NT):
                wge(t, "vZ1", i - 1)
                if i == 0:
                    t.wait_ge(S["vOev"], 64)
                wge(t, "vOev", 8 * (i // 4 + 1))
                base = 1024 + (i % 2) * 1024
                ins = None
                for c in range(8):
                    for nh in range(2):
                        ins = t.matmul(
                            psA[:, base + nh * 512: base + nh * 512 + 512],
                            actT[:, c, i * 128:(i + 1) * 128],
                            actU[:, c, 3072 + nh * 512: 3072 + nh * 512 + 512],
                            start=(c == 0), stop=False,
                            skip_group_check=True)
                for nh in range(2):
                    ins = t.matmul(
                        psA[:, base + nh * 512: base + nh * 512 + 512],
                        onesb[0:1, :],
                        bias_b[0:1, 3, nh * 512:nh * 512 + 512],
                        start=False, stop=True, skip_group_check=True)
                ins.then_inc(S["tZ"], 1)
                if lvl >= 4:
                    t.wait_ge(S["pXQ"], i + 1)
                    wge(t, "aXQe", i)
                    if i == 0:
                        t.wait_ge(S["vKV"], 1)
                        t.wait_ge(S["vSW"], NW)
                    ins = None
                    for c in range(8):
                        dst = bankbf3(6 + c // 4)
                        ins = t.transpose(dst[:, c % 4, :],
                                          xqbf[i % 2][:, c * 128:(c + 1) * 128],
                                          idn[:])
                    ins.then_inc(S["tXQT"], 1)

            if upto < 4:
                return
            GB = (P3A_GB * 4)[:4]
            UB = (P3A_UB * 4)[:4]
            t.wait_ge(S["vOev"], 64)
            t.wait_ge(S["vZ1"], NT)
            t.wait_ge(S["sX1"], 16 * NT)
            t.wait_ge(S["vKV"], 1)
            t.wait_ge(S["aXQe"], NT)
            for i in range(NT):
                t.wait_ge(S["dXQT"], 16 * (i + 1))
                xv = xqTi[i % 2].rearrange("p (a b) -> p a b", b=128)
                for os in range(8):
                    wge(t, "aGev", 8 * i + os - (len(P3A_GB) - 1))
                    if i == 0:
                        t.wait_ge(S["dWgT"], 128 * (os // 2 + 1))
                    ins = None
                    for c in range(8):
                        ins = t.matmul(bank(GB[os % len(P3A_GB)]), xv[:, c, :],
                                       actT[:, c, os * 512:(os + 1) * 512],
                                       start=(c == 0), stop=(c == 7),
                                       skip_group_check=True)
                    ins.then_inc(S["tG"], 1)
                    wge(t, "vUev", 8 * i + os - (len(P3A_UB) - 1))
                    if i == 0:
                        t.wait_ge(S["dWuT"], 128 * (os // 2 + 1))
                    ins = None
                    for c in range(8):
                        ins = t.matmul(bank(UB[os % len(P3A_UB)]), xv[:, c, :],
                                       actU[:, c, os * 512:(os + 1) * 512],
                                       start=(c == 0), stop=(c == 7),
                                       skip_group_check=True)
                    ins.then_inc(S["tU"], 1)

            if upto < 5:
                return
            t.wait_ge(S["vHI8"], NT)
            t.wait_ge(S["aGev"], 8 * NT)
            t.wait_ge(S["vUev"], 8 * NT)
            for i in range(NT):
                t.wait_ge(S["dHQ"], 16 * (i + 1))
                for r in range(8):
                    wge(t, "aHTe", 8 * i + r - 1)
                    dst = bankbf3(4 + r % 2)
                    ins = None
                    for k in range(4):
                        ins = t.transpose(dst[:, k, :],
                                          hqi[i % 2][:, (4 * r + k) * 128:
                                                     (4 * r + k + 1) * 128],
                                          idn[:])
                    ins.then_inc(S["tTr"], 1)
                t.wait_ge(S["aHTe"], 8 * (i + 1))
                hv = hqT[i % 2].rearrange("p (a b) -> p a b", b=128)
                for nh in range(2):
                    wge(t, "vDev", 2 * i + nh - 1)
                    ins = None
                    for c in range(32):
                        if i == 0 and nh == 0:
                            t.wait_ge(S["dWdT"], 16 * (c + 1))
                        ins = t.matmul(bank(6 + nh), hv[:, c, :],
                                       actTf[:, c * 1024 + nh * 512:
                                             c * 1024 + nh * 512 + 512],
                                       start=(c == 0), stop=(c == 31),
                                       skip_group_check=True)
                    ins.then_inc(S["tD"], 1)

        @block.vector
        def _(v):
            fen = [0]

            def vfence(ins):
                fen[0] += 1
                ins.then_inc(S["vFen"], 1)
                v.wait_ge(S["vFen"], fen[0])

            def vrecip(out, in_, pre_ins=None):
                if pre_ins is not None:
                    vfence(pre_ins)
                r = v.reciprocal(out, in_)
                vfence(r)
                return r
            for j in range(8):
                v.wait_ge(S["tBias"], j + 1)
                v.tensor_copy(bias_b[:, j // 2, (j % 2) * 512:(j % 2) * 512 + 512],
                              bank(j % 2)).then_inc(S["vBias"], 1)
            v.wait_ge(S["aSig"], 1)
            v.tensor_scalar_mul(bst[0:1, 1056:1056 + H],
                                bst[0:1, 1040:1040 + H],
                                LN4096).then_inc(S["vSig"], 1)
            v.wait_ge(S["aExp"], 1)
            vrecip(bst[0:1, 1088:1088 + H], bst[0:1, 1072:1072 + H])
            v.engine_nop().then_inc(S["vRin"], 1)
            v.wait_ge(S["tRsc"], 1)
            v.tensor_copy(rsc[:], psA[:, 3584:3584 + H]).then_inc(S["vRsc"], 1)

            def mean_chunk(j):
                v.wait_ge(S["dWm"], 16 * (j + 1))
                v.tensor_reduce(macc[:, j:j + 1].rearrange("p (a o) -> p a o", o=1),
                                wbuf[j % 2].rearrange("p (a b) -> p a b", a=1),
                                AX.X, ALU.add,
                                apply_absolute_value=True).then_inc(S["vMean"], 1)

            def mred_fin(w):
                v.tensor_reduce(mred[:, w:w + 1].rearrange("p (a o) -> p a o", o=1),
                                macc[:, w * NCH:(w + 1) * NCH]
                                .rearrange("p (a b) -> p a b", a=1),
                                AX.X, ALU.add).then_inc(S["vMred"], 1)

            def sw_fin(w):
                v.wait_ge(S["sMR"], 16 * (2 * w + 2))
                vfence(v.tensor_reduce(bst[0:1, 1104 + w:1105 + w]
                                       .rearrange("p (a o) -> p a o", o=1),
                                       bst[0:1, 1152:1280]
                                       .rearrange("p (a b) -> p a b", a=1),
                                       AX.X, ALU.add))
                vfence(v.tensor_scalar_mul(bst[0:1, 1108 + w:1109 + w],
                                           bst[0:1, 1104 + w:1105 + w], INV_N))
                mx = v.tensor_scalar_max(bst[0:1, 1112 + w:1113 + w],
                                         bst[0:1, 1108 + w:1109 + w], 1e-5)
                vrecip(bst[0:1, 1120 + 2 * w:1121 + 2 * w],
                       bst[0:1, 1112 + w:1113 + w], pre_ins=mx)
                vrecip(bst[0:1, 1121 + 2 * w:1122 + 2 * w],
                       bst[0:1, 1120 + 2 * w:1121 + 2 * w])
                v.engine_nop().then_inc(S["vSWr"], 1)
                v.wait_ge(S["tSWb"], w + 1)
                v.tensor_copy(swuw[:, 2 * w:2 * w + 2],
                              psA[:, 1540 + 2 * w:1542 + 2 * w]
                              ).then_inc(S["vSW"], 1)

            mean_emitted = 0
            fin_emitted = 0
            if upto == 0:
                for j in range(NQC):
                    mean_chunk(j)
                for w in range(NW):
                    mred_fin(w)
                    sw_fin(w)
                return
            for i in range(NT):
                v.wait_ge(S["tQ"], i + 1)
                v.tensor_tensor(qf[i % 2], psA[:, 0:1024], bias_b[:, 0, :],
                                ALU.add).then_inc(S["vQe"], 1)
                v.wait_ge(S["aSQq"], i + 1)
                v.tensor_reduce(rq[i % 2].rearrange("p (a o) -> p a o", o=1),
                                sqq.rearrange("p (a b) -> p a b", b=HD),
                                AX.X, ALU.add).then_inc(S["vRQ"], 1)
                v.wait_ge(S["tK"], i + 1)
                wge(v, "pKN", i - 1)
                v.tensor_tensor(kf[i % 2], psA[:, 1024:2048], bias_b[:, 1, :],
                                ALU.add).then_inc(S["vKe"], 1)
                v.wait_ge(S["aSQk"], i + 1)
                v.tensor_reduce(rk[i % 2].rearrange("p (a o) -> p a o", o=1),
                                sqk.rearrange("p (a b) -> p a b", b=HD),
                                AX.X, ALU.add).then_inc(S["vRK"], 1)
                v.wait_ge(S["tV"], i + 1)
                wge(v, "tKV", i - 1)
                v.tensor_tensor(vb[i % 2], psA[:, 2048:3072], bias_b[:, 2, :],
                                ALU.add).then_inc(S["vVB"], 1)
                v.wait_ge(S["aRTq"], i + 1)
                wge(v, "sQN", 16 * (i - 1))
                mx = v.tensor_scalar_max(nrm, srtq[i % 2], 1e-12)
                vrecip(rinvq, nrm, pre_ins=mx)
                a3, b3 = broadcast_tensor_aps(
                    qf[i % 2].rearrange("p (a b) -> p a b", b=HD),
                    rinvq.rearrange("p (a o) -> p a o", o=1))
                v.tensor_tensor(qn[i % 2].rearrange("p (a b) -> p a b", b=HD),
                                a3, b3, ALU.mult).then_inc(S["vQN"], 1)
                v.wait_ge(S["aRTk"], i + 1)
                wge(v, "tKV", i - 1)
                mx = v.tensor_scalar_max(nrm, srtk[i % 2], 1e-12)
                vrecip(rinvk[i % 2], nrm, pre_ins=mx)
                a3, b3 = broadcast_tensor_aps(
                    kf[i % 2].rearrange("p (a b) -> p a b", b=HD),
                    rinvk[i % 2].rearrange("p (a o) -> p a o", o=1))
                v.tensor_tensor(kn[i % 2].rearrange("p (a b) -> p a b", b=HD),
                                a3, b3, ALU.mult).then_inc(S["pKN"], 1)
                while mean_emitted < NQC and MEAN_TILE[mean_emitted] <= i:
                    mean_chunk(mean_emitted)
                    mean_emitted += 1
                while fin_emitted < NW and FIN_TILE[fin_emitted] <= i:
                    mred_fin(fin_emitted)
                    fin_emitted += 1
            v.wait_ge(S["tKV"], NT)
            ins = None
            for h in range(H):
                r0 = 64 * (h % 2)
                col = 3072 + 64 * h
                ins = v.tensor_scalar_mul(kvbf[r0:r0 + 64, h * HD:(h + 1) * HD],
                                          psA[r0:r0 + 64, col:col + 64],
                                          rsc[r0:r0 + 64, h:h + 1])
            ins.then_inc(S["vKV"], 1)
            for w in range(NW):
                sw_fin(w)
            if upto < 2:
                return
            for s in range(8):
                for c in range(8):
                    cs = 8 * s + c
                    v.wait_ge(S["tO"], cs + 1)
                    v.tensor_copy(actT[:, c, s * 512:(s + 1) * 512],
                                  bank(cs % 4)).then_inc(S["vOev"], 1)
            if upto < 3:
                return
            for i in range(NT):
                base = 1024 + (i % 2) * 1024
                v.wait_ge(S["tZ"], i + 1)
                v.wait_ge(S["dXin"], 16 * (i + 1))
                v.tensor_tensor(z2, psA[:, base:base + 1024], xres[i % 2],
                                ALU.add).then_inc(S["vZ1"], 1)
                if lvl < 2:
                    continue
                v.tensor_reduce(st2c(0).rearrange("p (a o) -> p a o", o=1),
                                z2.rearrange("p (a b) -> p a b", a=1),
                                AX.X, ALU.add)
                v.wait_ge(S["aSQ2"], i + 1)
                vfence(v.tensor_reduce(st2c(1).rearrange("p (a o) -> p a o", o=1),
                                       sq2.rearrange("p (a b) -> p a b", a=1),
                                       AX.X, ALU.add))
                v.tensor_scalar_mul(st2c(2), st2c(0), 1.0 / D)
                vfence(v.tensor_scalar_mul(st2c(3), st2c(1), 1.0 / D))
                vfence(v.tensor_tensor(st2c(4), st2c(2), st2c(2), ALU.mult))
                v.tensor_tensor(st2c(5), st2c(3), st2c(4),
                                ALU.subtract).then_inc(S["vVar2"], 1)
                v.wait_ge(S["aRS2"], i + 1)
                wge(v, "sX1", 16 * (i - 1))
                vrecip(st2c(6), st2c(7))
                v.tensor_scalar(x1b[i % 2], z2, st2c(2), st2c(6),
                                ALU.subtract, ALU.mult).then_inc(S["vX1"], 1)
                if lvl < 3:
                    continue
                vfence(v.tensor_reduce(st2c(8).rearrange("p (a o) -> p a o", o=1),
                                       x1b[i % 2].rearrange("p (a b) -> p a b", a=1),
                                       AX.X, ALU.max, apply_absolute_value=True))
                mx = v.tensor_scalar_max(st2c(9), st2c(8), 1e-5)
                vrecip(st2c(11), st2c(9), pre_ins=mx)
                ml = v.tensor_scalar_mul(st2c(10), st2c(11), 127.0)
                vrecip(sxs[:, i:i + 1], st2c(10), pre_ins=ml)
                vfence(v.tensor_scalar_mul(sq2, x1b[i % 2], st2c(10)))
                v.tensor_copy(xi8, sq2).then_inc(S["vXI8"], 1)
                if lvl < 4:
                    continue
                wge(v, "tXQT", i)
                v.tensor_copy(xqbf[i % 2], xi8).then_inc(S["pXQ"], 1)
                for j in P2B_QUANT.get(i, []):
                    v.wait_ge(S["aWM"], j + 1)
                    wge(v, "sWQ", 16 * j)
                    v.tensor_copy(i8q, wbuf[j % 2])
                    v.tensor_copy(wqbf[0], i8q)
                    v.tensor_scalar(wqbf[0], wqbf[0], -1.0, 1.0,
                                    ALU.max, ALU.min).then_inc(S["pQbf"], 1)
            if upto < 4:
                return
            v.wait_ge(S["sWQ"], 16 * NQC)
            for i in range(NT):
                wge(v, "aGev", 8 * i)
                v.tensor_tensor(sgu[:, 0:1], sxs[:, i:i + 1], swuw[:, 1:2],
                                ALU.mult)
                vfence(v.tensor_tensor(sgu[:, 1:2], sxs[:, i:i + 1],
                                       swuw[:, 3:4], ALU.mult))
                v.engine_nop().then_inc(S["vSG"], 1)
                for os in range(8):
                    v.wait_ge(S["tU"], 8 * i + os + 1)
                    v.wait_ge(S["aGev"], 8 * i + os + 1)
                    ins = v.scalar_tensor_tensor(hbuf[:, os * 512:(os + 1) * 512],
                                                 bank((P3A_UB * 4)[os % len(P3A_UB)]),
                                                 sgu[:, 1:2],
                                                 gbuf[:, os * 512:(os + 1) * 512],
                                                 ALU.mult, ALU.mult)
                    if os == 7:
                        vfence(ins)
                        v.engine_nop().then_inc(S["vUev"], 1)
                    else:
                        ins.then_inc(S["vUev"], 1)
                vfence(v.tensor_reduce(st3c(0).rearrange("p (a o) -> p a o", o=1),
                                       hbuf.rearrange("p (a b) -> p a b", a=1),
                                       AX.X, ALU.max, apply_absolute_value=True))
                mx = v.tensor_scalar_max(st3c(1), st3c(0), 1e-5)
                vrecip(st3c(3), st3c(1), pre_ins=mx)
                ml = v.tensor_scalar_mul(st3c(2), st3c(3), 127.0)
                vrecip(sds[:, i:i + 1], st3c(2), pre_ins=ml)
                v.engine_nop().then_inc(S["vSH"], 1)
                v.wait_ge(S["aHM"], i + 1)
                v.tensor_copy(hi8, gbuf).then_inc(S["vHI8"], 1)
                wge(v, "sHQ", 16 * (i - 1))
                v.tensor_copy(hqb[i % 2], hi8).then_inc(S["pHQ"], 1)
                for j in P3A_QUANT.get(i, []):
                    v.wait_ge(S["aWM"], j + 1)
                    wge(v, "sWQ", 16 * j)
                    v.tensor_copy(i8q, wbuf[j % 2])
                    v.tensor_copy(wqbf[0], i8q)
                    v.tensor_scalar(wqbf[0], wqbf[0], -1.0, 1.0,
                                    ALU.max, ALU.min).then_inc(S["pQbf"], 1)
            if upto < 5:
                return
            for i in range(NT):
                vfence(v.tensor_tensor(sgu[:, 2:3], sds[:, i:i + 1],
                                       swuw[:, 5:6], ALU.mult))
                v.wait_ge(S["dX1"], 16 * (i + 1))
                for nh in range(2):
                    v.wait_ge(S["tD"], 2 * i + nh + 1)
                    v.scalar_tensor_tensor(z3[:, nh * 512:(nh + 1) * 512],
                                           bank(6 + nh), sgu[:, 2:3],
                                           x1i[i % 2][:, nh * 512:(nh + 1) * 512],
                                           ALU.mult, ALU.add
                                           ).then_inc(S["vDev"], 1)
                v.tensor_reduce(st4c(0).rearrange("p (a o) -> p a o", o=1),
                                z3.rearrange("p (a b) -> p a b", a=1),
                                AX.X, ALU.add)
                v.wait_ge(S["aSQ3"], i + 1)
                vfence(v.tensor_reduce(st4c(1).rearrange("p (a o) -> p a o", o=1),
                                       sq3.rearrange("p (a b) -> p a b", a=1),
                                       AX.X, ALU.add))
                v.tensor_scalar_mul(st4c(2), st4c(0), 1.0 / D)
                vfence(v.tensor_scalar_mul(st4c(3), st4c(1), 1.0 / D))
                vfence(v.tensor_tensor(st4c(4), st4c(2), st4c(2), ALU.mult))
                v.tensor_tensor(st4c(5), st4c(3), st4c(4),
                                ALU.subtract).then_inc(S["vVar3"], 1)
                v.wait_ge(S["aRS3"], i + 1)
                wge(v, "sOUT", 16 * (i - 1))
                vrecip(st4c(6), st4c(7))
                v.tensor_scalar(outb[i % 2], z3, st4c(2), st4c(6),
                                ALU.subtract, ALU.mult).then_inc(S["vOut"], 1)

        @block.scalar
        def _(s):
            s.wait_ge(S["dBias"], 16 * 5)
            s.activation(bst[0:1, 1040:1040 + H], bst[0:1, 1024:1024 + H],
                         AF.Sigmoid).then_inc(S["aSig"], 1)
            s.wait_ge(S["vSig"], 1)
            s.activation(bst[0:1, 1072:1072 + H], bst[0:1, 1056:1056 + H],
                         AF.Exp).then_inc(S["aExp"], 1)

            qwj = [0]

            def qwrites(upto):
                while qwj[0] < upto:
                    j = qwj[0]
                    s.wait_ge(S["pQbf"], j + 1)
                    dst, src = wsb_out(j)
                    s.dma_start(out=dst, in_=src).then_inc(S["sWQ"], 16)
                    qwj[0] += 1

            def mr_bounce():
                for w in range(NW):
                    s.wait_ge(S["vMred"], w + 1)
                    wge(s, "vSWr", w)
                    s.dma_start(out=mrsc[w, :],
                                in_=mred[:, w:w + 1]).then_inc(S["sMR"], 16)
                    s.wait_ge(S["sMR"], 16 * (2 * w + 1))
                    s.dma_start(out=bst[0:1, 1152:1280],
                                in_=mrsc[w:w + 1, :]).then_inc(S["sMR"], 16)

            if upto == 0:
                mr_bounce()
                return
            for i in range(NT):
                s.wait_ge(S["vQe"], i + 1)
                wge(s, "vRQ", i)
                s.activation(sqq, qf[i % 2], AF.Square).then_inc(S["aSQq"], 1)
                s.wait_ge(S["vRQ"], i + 1)
                wge(s, "vQN", i - 1)
                s.activation(srtq[i % 2], rq[i % 2], AF.Sqrt).then_inc(S["aRTq"], 1)
                s.wait_ge(S["vKe"], i + 1)
                wge(s, "vRK", i)
                s.activation(sqk, kf[i % 2], AF.Square).then_inc(S["aSQk"], 1)
                s.wait_ge(S["vRK"], i + 1)
                wge(s, "pKN", i - 1)
                s.activation(srtk[i % 2], rk[i % 2], AF.Sqrt).then_inc(S["aRTk"], 1)
                s.wait_ge(S["vQN"], i + 1)
                s.dma_start(out=qsc[i * 128:(i + 1) * 128, :],
                            in_=qn[i % 2]).then_inc(S["sQN"], 16)
                qwrites(sum(len(P1_QUANT.get(k, [])) for k in range(i + 1)))
            mr_bounce()
            if upto < 3:
                s.wait_ge(S["sQN"], 16 * NT)
                return
            qwrites(N_P1Q + len(P2A_QUANT))
            if dbg and upto >= 2:
                s.wait_ge(S["vKV"], 1)
                s.dma_start(out=dbg_kv, in_=kvbf[:]).then_inc(S["sDBG"], 16)
                s.wait_ge(S["vOev"], 64)
                s.dma_start(out=dbg_ofm, in_=actT[:]).then_inc(S["sDBG"], 16)
                s.wait_ge(S["vSW"], NW)
                s.dma_start(out=dbg_macc, in_=arena[:, 9408:9459]
                            ).then_inc(S["sDBG"], 16)
                s.dma_start(out=dbg_swuw, in_=swuw[:]).then_inc(S["sDBG"], 16)
                s.dma_start(out=dbg_bst, in_=bst[0:1, 1088:1152]
                            ).then_inc(S["sDBG"], 16)
            for i in range(NT):
                if lvl >= 2:
                    s.wait_ge(S["vZ1"], i + 1)
                    wge(s, "vXI8" if lvl >= 3 else "vVar2", i)
                    s.activation(sq2, z2, AF.Square).then_inc(S["aSQ2"], 1)
                    s.wait_ge(S["vVar2"], i + 1)
                    s.activation(st2c(7), st2c(5), AF.Sqrt,
                                 bias=cst[:, 1:2]).then_inc(S["aRS2"], 1)
                    s.wait_ge(S["vX1"], i + 1)
                    s.dma_start(out=x1sc[i * 128:(i + 1) * 128, :],
                                in_=x1b[i % 2]).then_inc(S["sX1"], 16)
                if lvl >= 4:
                    s.wait_ge(S["tXQT"], i + 1)
                    wge(s, "sXQT", 16 * (i - 1))
                    xv = xqTt[i % 2].rearrange("p (a b) -> p a b", b=128)
                    s.copy(xv[:, 0:4, :], bankbf3(6)[:, 0:4, :])
                    s.copy(xv[:, 4:8, :],
                           bankbf3(7)[:, 0:4, :]).then_inc(S["aXQe"], 1)
                    s.dma_start(out=xqT_sc.transpose([1, 0, 2])[:, :, i * 128:(i + 1) * 128],
                                in_=xqTt[i % 2].rearrange("p (a b) -> p a b", b=128)
                                ).then_inc(S["sXQT"], 16)
                for j in P2B_QUANT.get(i, []):
                    w = j // NCH
                    s.wait_ge(S["dWq"], 16 * (j + 1))
                    s.wait_ge(S["vSW"], w + 1)
                    wge(s, "pQbf", j - 1)
                    s.activation(wbuf[j % 2], wbuf[j % 2], AF.Copy,
                                 scale=swuw[:, 2 * w:2 * w + 1]
                                 ).then_inc(S["aWM"], 1)
                qwrites(N_P1Q + len(P2A_QUANT)
                        + sum(len(P2B_QUANT.get(k, [])) for k in range(i + 1)))
            qwrites(NQC)
            if upto < 4:
                if lvl >= 2:
                    s.wait_ge(S["sX1"], 16 * NT)
                if lvl >= 4:
                    s.wait_ge(S["sXQT"], 16 * NT)
                s.wait_ge(S["sWQ"], 16 * NQC)
                return
            for i in range(NT):
                for os in range(8):
                    s.wait_ge(S["tG"], 8 * i + os + 1)
                    s.wait_ge(S["vSG"], i + 1)
                    if os == 0:
                        wge(s, "vHI8", i)
                    wge(s, "vUev", 8 * (i - 1) + os + 1)
                    s.activation(gbuf[:, os * 512:(os + 1) * 512],
                                 bank((P3A_GB * 4)[os % len(P3A_GB)]), AF.Silu,
                                 scale=sgu[:, 0:1]).then_inc(S["aGev"], 1)
                s.wait_ge(S["vSH"], i + 1)
                s.wait_ge(S["vUev"], 8 * (i + 1))
                s.activation(gbuf, hbuf, AF.Copy,
                             scale=st3c(2)).then_inc(S["aHM"], 1)
                for j in P3A_QUANT.get(i, []):
                    w = j // NCH
                    s.wait_ge(S["dWq"], 16 * (j + 1))
                    s.wait_ge(S["vSW"], w + 1)
                    wge(s, "pQbf", j - 1)
                    s.activation(wbuf[j % 2], wbuf[j % 2], AF.Copy,
                                 scale=swuw[:, 2 * w:2 * w + 1]
                                 ).then_inc(S["aWM"], 1)
                s.wait_ge(S["pHQ"], i + 1)
                s.dma_start(out=hqsc[i * 128:(i + 1) * 128, :],
                            in_=hqb[i % 2]).then_inc(S["sHQ"], 16)
                qwrites(32 + sum(len(P3A_QUANT.get(k, [])) for k in range(i + 1)))
            if upto < 5:
                s.wait_ge(S["sHQ"], 16 * NT)
                return
            qwrites(NQC)
            def x1_load(i):
                if i == 0:
                    s.wait_ge(S["vHI8"], NT)
                s.wait_ge(S["sX1"], 16 * (i + 1))
                wge(s, "vDev", 2 * (i - 1))
                s.dma_start(out=x1i[i % 2],
                            in_=x1sc[i * 128:(i + 1) * 128, :]
                            ).then_inc(S["dX1"], 16)

            def hq_evacs(i):
                for r in range(8):
                    s.wait_ge(S["tTr"], 8 * i + r + 1)
                    if r == 0:
                        wge(s, "tD", 2 * (i - 1))
                    hv = hqT[i % 2].rearrange("p (a b) -> p a b", b=128)
                    s.copy(hv[:, 4 * r:4 * r + 4, :],
                           bankbf3(4 + r % 2)[:, 0:4, :]).then_inc(S["aHTe"], 1)

            x1_load(0)
            x1_load(1)
            hq_evacs(0)
            for i in range(NT):
                if i + 1 < NT:
                    hq_evacs(i + 1)
                s.wait_ge(S["vDev"], 2 * i + 2)
                wge(s, "vVar3", i)
                s.activation(sq3, z3, AF.Square).then_inc(S["aSQ3"], 1)
                if i + 2 < NT:
                    x1_load(i + 2)
                s.wait_ge(S["vVar3"], i + 1)
                s.activation(st4c(7), st4c(5), AF.Sqrt,
                             bias=cst[:, 1:2]).then_inc(S["aRS3"], 1)
                s.wait_ge(S["vOut"], i + 1)
                s.dma_start(out=out_h[i * 128:(i + 1) * 128, :],
                            in_=outb[i % 2]).then_inc(S["sOUT"], 16)
            s.wait_ge(S["sOUT"], 16 * NT)

        stack.close()
    return nc


_NC_CACHE = []


def kernel(L0_attn_x, L0_attn_Wq, L0_attn_bq, L0_attn_Wk, L0_attn_bk,
           L0_attn_Wv, L0_attn_bv, L0_attn_Wo, L0_attn_bo, L0_attn_m,
           L0_ffn_w_gate, L0_ffn_w_up, L0_ffn_w_down):
    if not _NC_CACHE:
        _NC_CACHE.append(build())
    nc = _NC_CACHE[0]

    def f(a):
        return np.ascontiguousarray(np.asarray(a, dtype=np.float32))

    X = f(L0_attn_x)
    shared = {
        "wq": f(L0_attn_Wq), "wk": f(L0_attn_Wk), "wv": f(L0_attn_Wv),
        "wo": f(L0_attn_Wo), "bq": f(L0_attn_bq), "bk": f(L0_attn_bk),
        "bv": f(L0_attn_bv), "bo": f(L0_attn_bo), "m": f(L0_attn_m),
        "wg": f(L0_ffn_w_gate), "wu": f(L0_ffn_w_up), "wd": f(L0_ffn_w_down),
    }
    in_maps = [dict(shared, x=X[i]) for i in range(NCHIP)]
    res = run_bass_kernel_spmd(nc, in_maps, core_ids=list(range(NCHIP)))
    return np.stack([np.asarray(res.results[i]["out"], dtype=np.float32)
                     for i in range(NCHIP)], axis=0)


# version 10
# speedup vs baseline: 1.3193x; 1.1020x over previous
import math
import contextlib
import numpy as np
import concourse.bass as bass
import concourse.mybir as mybir
from concourse.bass import broadcast_tensor_aps
from concourse.bass_utils import run_bass_kernel_spmd

F32 = mybir.dt.float32
BF16 = mybir.dt.bfloat16
I8 = mybir.dt.int8
AF = mybir.ActivationFunctionType
ALU = mybir.AluOpType
AX = mybir.AxisListType

T = 4096
NT = 32
D = 1024
H = 16
HD = 64
NI = 4096
NCHIP = 8
INV_N = 1.0 / 4194304.0
LN4096 = math.log(4096.0)
MAGIC = 12582912.0

NCH = 16
NW = 3
NQC = NW * NCH

MEAN_TILE = [min(31, int(j / 1.55)) for j in range(NQC)]
FIN_TILE = [MEAN_TILE[NCH - 1] + 1, MEAN_TILE[2 * NCH - 1] + 1, 31]
QSLOT = {j: 33 + (2 * j) // 3 for j in range(NQC)}
P1_QUANT = {i: [] for i in range(32)}
P2A_QUANT = []
P2B_QUANT = {i: [j for j, s in QSLOT.items() if s == 33 + i] for i in range(32)}
P3A_QUANT = {i: [] for i in range(32)}
N_P1Q = 0


P2B_LVL = 4
P3A_DEPTH = 4
P3A_GB = [0, 1, 4, 5]
P3A_UB = [2, 3, 6, 7]
P3B_SHIFT = False


def build(upto=5, dbg=False):
    nc = bass.Bass("TRN2", target_bir_lowering=False)
    lvl = P2B_LVL if upto == 3 else 4
    scratch_kind = "ExternalOutput" if dbg else "Internal"

    x_h = nc.dram_tensor("x", [T, D], F32, kind="ExternalInput").ap()
    wq_h = nc.dram_tensor("wq", [D, D], F32, kind="ExternalInput").ap()
    wk_h = nc.dram_tensor("wk", [D, D], F32, kind="ExternalInput").ap()
    wv_h = nc.dram_tensor("wv", [D, D], F32, kind="ExternalInput").ap()
    wo_h = nc.dram_tensor("wo", [D, D], F32, kind="ExternalInput").ap()
    bq_h = nc.dram_tensor("bq", [D], F32, kind="ExternalInput").ap()
    bk_h = nc.dram_tensor("bk", [D], F32, kind="ExternalInput").ap()
    bv_h = nc.dram_tensor("bv", [D], F32, kind="ExternalInput").ap()
    bo_h = nc.dram_tensor("bo", [D], F32, kind="ExternalInput").ap()
    m_h = nc.dram_tensor("m", [H], F32, kind="ExternalInput").ap()
    wg_h = nc.dram_tensor("wg", [NI, D], F32, kind="ExternalInput").ap()
    wu_h = nc.dram_tensor("wu", [NI, D], F32, kind="ExternalInput").ap()
    wd_h = nc.dram_tensor("wd", [D, NI], F32, kind="ExternalInput").ap()
    out_h = nc.dram_tensor("out", [T, D], F32, kind="ExternalOutput").ap()

    xbf = nc.dram_tensor("xbf", [T, D], BF16).ap()
    xqsc = nc.dram_tensor("xqsc", [T, D], BF16).ap()
    qsc = nc.dram_tensor("qsc", [T, D], BF16, kind=scratch_kind).ap()
    xqT_sc = nc.dram_tensor("xqT_sc", [8, 128, T], BF16, kind=scratch_kind).ap()
    x1sc = nc.dram_tensor("x1sc", [T, D], F32, kind=scratch_kind).ap()
    hqsc = nc.dram_tensor("hqsc", [T, NI], BF16, kind=scratch_kind).ap()
    wgq = nc.dram_tensor("wgq", [NI, D], BF16, kind=scratch_kind).ap()
    wuq = nc.dram_tensor("wuq", [NI, D], BF16).ap()
    wdq = nc.dram_tensor("wdq", [D, NI], BF16).ap()
    mrsc = nc.dram_tensor("mrsc", [NW, 128], F32).ap()
    if dbg:
        dbg_kv = nc.dram_tensor("dbg_kv", [128, D], BF16,
                                kind="ExternalOutput").ap()
        dbg_ofm = nc.dram_tensor("dbg_ofm", [128, 8, T], BF16,
                                 kind="ExternalOutput").ap()
        dbg_macc = nc.dram_tensor("dbg_macc", [128, 51], F32,
                                  kind="ExternalOutput").ap()
        dbg_swuw = nc.dram_tensor("dbg_swuw", [128, 8], F32,
                                  kind="ExternalOutput").ap()
        dbg_bst = nc.dram_tensor("dbg_bst", [1, 64], F32,
                                 kind="ExternalOutput").ap()
    wq_src = [wg_h, wu_h, wd_h]
    wq_dst = [wgq, wuq, wdq]

    def wchunk(ap, j):
        r, c = ap.shape
        v = ap.rearrange("(a k) n -> k a n", k=128)
        if c >= 2048:
            n2 = c // 2048
            return v[:, j // n2, (j % n2) * 2048:(j % n2) * 2048 + 2048], "2d"
        na = 2048 // c
        return v[:, j * na:(j + 1) * na, :], "3d"

    with (
        nc.sbuf_tensor("actT", [128, 8, T], BF16) as actT,
        nc.sbuf_tensor("actU", [128, 8, T], BF16) as actU,
        nc.sbuf_tensor("bias_b", [128, 4, D], BF16) as bias_b,
        nc.sbuf_tensor("kvbf", [128, D], BF16) as kvbf,
        nc.sbuf_tensor("rsc", [128, H], F32) as rsc,
        nc.sbuf_tensor("sxs", [128, NT], F32) as sxs,
        nc.sbuf_tensor("sds", [128, NT], F32) as sds,
        nc.sbuf_tensor("swuw", [128, 8], F32) as swuw,
        nc.sbuf_tensor("cst", [128, 6], F32) as cst,
        nc.sbuf_tensor("ones", [1, 128], F32) as ones,
        nc.sbuf_tensor("onesb", [1, 128], BF16) as onesb,
        nc.sbuf_tensor("bst", [1, 1280], F32) as bst,
        nc.sbuf_tensor("idn", [128, 128], BF16) as idn,
        nc.sbuf_tensor("arena", [128, 16160], F32) as arena,
        nc.psum_tensor("psA", [128, 4096], F32) as psA,
        nc.Block() as block,
    ):
        def af32(off, n):
            assert off + n <= 16160, (off, n)
            return arena[:, off:off + n]

        def abf(off, nf32):
            return af32(off, nf32).bitcast(BF16)

        def ai8(off, nf32):
            return af32(off, nf32).bitcast(I8)

        qf = [af32(0, 1024), af32(1024, 1024)]
        kf = [af32(2048, 1024), af32(3072, 1024)]
        sqq = af32(4096, 1024)
        sqk = af32(5120, 1024)
        qn = [abf(6144, 512), abf(6656, 512)]
        kn = [abf(7168, 512), abf(7680, 512)]
        vb = [abf(8192, 512), abf(8704, 512)]
        rq = [af32(9216, 16), af32(9232, 16)]
        rk = [af32(9248, 16), af32(9264, 16)]
        srtq = [af32(9280, 16), af32(9296, 16)]
        srtk = [af32(9312, 16), af32(9328, 16)]
        nrm = af32(9344, 16)
        rinvq = af32(9360, 16)
        rinvk = [af32(9376, 16), af32(9392, 16)]
        macc = af32(9408, 48)
        mred = af32(9456, 3)
        wbuf = [af32(9472, 2048), af32(11520, 2048)]
        i8q = ai8(13568, 512)
        wqbf = [abf(14080, 1024)]

        def wsb(j):
            ap, kind = wchunk(wq_src[j // NCH], j % NCH)
            if kind == "2d":
                return ap, wbuf[j % 2]
            return ap, wbuf[j % 2].rearrange("p (a b) -> p a b", b=1024)

        def wsb_out(j):
            ap, kind = wchunk(wq_dst[j // NCH], j % NCH)
            if kind == "2d":
                return ap, wqbf[0]
            return ap, wqbf[0].rearrange("p (a b) -> p a b", b=1024)

        xres = [af32(0, 1024), af32(1024, 1024)]
        z2 = af32(2048, 1024)
        sq2 = af32(3072, 1024)
        x1b = [af32(4096, 1024), af32(5120, 1024)]
        xqbf = [abf(6144, 512), abf(6656, 512)]
        xi8 = ai8(7168, 256)
        st2 = af32(8448, 32)
        xqTt = [abf(7424, 512), abf(7936, 512)]

        def st2c(k):
            return st2[:, k:k + 1]

        gbuf = af32(1024, 4096)
        hbuf = af32(5120, 4096)
        hqb = [abf(9216, 2048), abf(11264, 2048)]
        hi8 = ai8(13312, 1024)
        st3 = af32(14336, 32)
        sgu = af32(14368, 4)
        xqTi = [abf(15104, 512), abf(15616, 512)]

        def st3c(k):
            return st3[:, k:k + 1]

        hqi = [abf(0, 2048), abf(2048, 2048)]
        hqT = [abf(4096, 2048), abf(6144, 2048)]
        x1i = [af32(8192, 1024), af32(9216, 1024)]
        z3 = af32(10240, 1024)
        sq3 = af32(11264, 1024)
        outb = [af32(12288, 1024), af32(13312, 1024)]
        st4 = af32(16128, 32)

        def st4c(k):
            return st4[:, k:k + 1]

        actTf = actT[:].rearrange("p a b -> p (a b)")

        def bank(b):
            return psA[:, b * 512:(b + 1) * 512]

        def bankbf3(b):
            return psA[:, b * 512:(b + 1) * 512].bitcast(BF16).rearrange(
                "p (a b) -> p a b", b=128)

        semnames = [
            "dBias", "dWm", "dXT", "dWq", "dQT", "dXin", "dWgT", "dWuT",
            "dXQT", "dWdT", "dHQ", "dX1", "gLoad", "sMR",
            "sQN", "sWQ", "sX1", "sXQT", "sHQ", "sOUT", "sXQ", "dXB", "sXB",
            "tBias", "tRsc", "tSum", "tSWb", "tQ", "tK", "tV", "tKV",
            "tO", "tZ", "tXQT", "tG", "tU", "tTr", "tD",
            "vBias", "vSig", "vRin", "vRsc", "vMean", "vMred", "vSWr",
            "vSW", "vQe", "vRQ", "vKe", "vRK", "vVB", "vQN", "vRKi",
            "vKV", "vOev", "vZ1", "vVar2", "vX1", "vXI8", "vXQTe",
            "vSG", "vUev", "vSH", "vHI8", "vHTe", "vDev", "vVar3", "vOut",
            "aSig", "aExp", "aSQq", "aRTq", "aSQk", "aRTk", "aSQ2",
            "aRS2", "aGev", "aHTe", "aSQ3", "aRS3", "aWM", "aHM", "aXQe",
            "pConst", "pKN", "pZ", "pXQ", "pQbf", "pHm", "pHQ", "sDBG",
            "vFen",
        ]
        stack = contextlib.ExitStack()
        S = {n: stack.enter_context(nc.semaphore(n)) for n in semnames}

        def wge(eng, name, v):
            if v > 0:
                eng.wait_ge(S[name], v)

        @block.sync
        def _(sync):
            for j, bh in enumerate([bq_h, bk_h, bv_h, bo_h]):
                wge(sync, "tBias", 2 * j)
                sync.dma_start(out=bst[0:1, 0:D],
                               in_=bh.rearrange("(o n) -> o n", o=1)
                               ).then_inc(S["dBias"], 16)
            sync.dma_start(out=bst[0:1, 1024:1024 + H],
                           in_=m_h.rearrange("(o n) -> o n", o=1)
                           ).then_inc(S["dBias"], 16)
            for c in range(8):
                sync.wait_ge(S["gLoad"], 16 * (c + 1))
                sync.dma_start(out=actT[:, c, :],
                               in_=xbf[:, c * 128:(c + 1) * 128],
                               transpose=True).then_inc(S["dXT"], 16)

            def qT_batch(qt):
                wge(sync, "tV", 8 * (qt + 1) if qt < 3 else NT)
                sync.wait_ge(S["sQN"], 16 * 8 * (qt + 1))
                for c in range(8):
                    sync.dma_start(out=actT[:, c, qt * 1024:(qt + 1) * 1024],
                                   in_=qsc[qt * 1024:(qt + 1) * 1024,
                                           c * 128:(c + 1) * 128],
                                   transpose=True).then_inc(S["dQT"], 16)

            QT_AT = {15: 0, 27: 1, 40: 2}
            for j in range(NQC):
                wge(sync, "vMean", j - 1)
                src, dst = wsb(j)
                sync.dma_start(out=dst, in_=src).then_inc(S["dWm"], 16)
                if upto >= 2 and j in QT_AT:
                    qT_batch(QT_AT[j])
            if upto < 2:
                return
            qT_batch(3)
            if upto < 3:
                return

            def wgT_batch(qt):
                wge(sync, "tZ", 8 * (qt + 1) if qt < 3 else NT)
                if dbg:
                    sync.wait_ge(S["sDBG"], 32)
                sync.wait_ge(S["sWQ"], 16 * (4 * qt + 4))
                for c in range(8):
                    sync.dma_start(out=actT[:, c, qt * 1024:(qt + 1) * 1024],
                                   in_=wgq[qt * 1024:(qt + 1) * 1024,
                                           c * 128:(c + 1) * 128],
                                   transpose=True).then_inc(S["dWgT"], 16)

            def wuT_batch(qt):
                if qt == 3:
                    sync.wait_ge(S["tZ"], NT)
                sync.wait_ge(S["sWQ"], 16 * (16 + 4 * qt + 4))
                for c in range(8):
                    sync.dma_start(out=actU[:, c, qt * 1024:(qt + 1) * 1024],
                                   in_=wuq[qt * 1024:(qt + 1) * 1024,
                                           c * 128:(c + 1) * 128],
                                   transpose=True).then_inc(S["dWuT"], 16)

            def xqTi_load(i):
                wge(sync, "sXQT", 16 * (i + 1))
                wge(sync, "tU", 8 * (i - 1))
                sync.dma_start(out=xqTi[i % 2].rearrange("p (a b) -> p a b", b=128),
                               in_=xqT_sc.transpose([1, 0, 2])[:, :, i * 128:(i + 1) * 128]
                               ).then_inc(S["dXQT"], 16)

            for i in range(NT):
                wge(sync, "vZ1", i - 1)
                sync.dma_start(out=xres[i % 2],
                               in_=x_h[i * 128:(i + 1) * 128, :]
                               ).then_inc(S["dXin"], 16)
                if upto >= 4:
                    if i == 9:
                        wgT_batch(0)
                    elif i == 10:
                        xqTi_load(0)
                    elif i == 12:
                        xqTi_load(1)
                    elif i == 13:
                        wuT_batch(0)
                    elif i == 17:
                        wgT_batch(1)
                    elif i == 19:
                        wuT_batch(1)
                    elif i == 25:
                        wgT_batch(2)
                    elif i == 27:
                        wuT_batch(2)
            if upto < 4:
                return
            wgT_batch(3)
            wuT_batch(3)
            for i in range(2, NT):
                xqTi_load(i)
            if upto < 5:
                return
            def hqi_load(i):
                if i == 0:
                    sync.wait_ge(S["pHQ"], NT)
                sync.wait_ge(S["sHQ"], 16 * (i + 1))
                wge(sync, "tTr", 8 * (i - 1))
                sync.dma_start(out=hqi[i % 2],
                               in_=hqsc[i * 128:(i + 1) * 128, :]
                               ).then_inc(S["dHQ"], 16)

            sync.wait_ge(S["tU"], 8 * NT)
            for c in range(32):
                sync.dma_start(out=actTf[:, c * 1024:(c + 1) * 1024],
                               in_=wdq[:, c * 128:(c + 1) * 128],
                               transpose=True).then_inc(S["dWdT"], 16)
                if c == 7:
                    hqi_load(0)
                elif c == 15:
                    hqi_load(1)
            for i in range(2, NT):
                hqi_load(i)

        @block.gpsimd
        def _(g):
            for c in range(8):
                g.dma_start(out=xbf[:, c * 128:(c + 1) * 128],
                            in_=x_h[:, c * 128:(c + 1) * 128]
                            ).then_inc(S["gLoad"], 16)
            for j, wh in enumerate([wq_h, wk_h, wv_h, wo_h]):
                g.dma_start(out=actU[:, :, j * 1024:(j + 1) * 1024],
                            in_=wh.rearrange("(c k) n -> k c n", k=128)
                            ).then_inc(S["gLoad"], 16)
            g.memset(ones[:], 1.0)
            g.memset(onesb[:], 1.0)
            g.memset(cst[:, 0:1], 127.0)
            g.memset(cst[:, 1:2], 1e-5)
            g.memset(cst[:, 2:3], 1e-12)
            g.memset(cst[:, 3:4], 1.0)
            g.memset(cst[:, 4:5], -1.0)
            g.memset(idn[:], 0.0)
            g.affine_select(out=idn[:], in_=idn[:], compare_op=ALU.not_equal,
                            fill=1.0, base=0, pattern=[[-1, 128]],
                            channel_multiplier=1).then_inc(S["pConst"], 1)

            if upto < 3:
                return
            for j in range(NQC):
                if j < 2:
                    g.wait_ge(S["vMean"], NQC)
                else:
                    wge(g, "pQbf", j - 1)
                src, dst = wsb(j)
                g.dma_start(out=dst, in_=src).then_inc(S["dWq"], 16)

        @block.tensor
        def _(t):
            t.wait_ge(S["pConst"], 1)
            for j in range(8):
                t.wait_ge(S["dBias"], 16 * (j // 2 + 1))
                wge(t, "vBias", j - 1)
                t.matmul(bank(j % 2), ones[0:1, :],
                         bst[0:1, (j % 2) * 512:(j % 2) * 512 + 512],
                         start=True, stop=True,
                         skip_group_check=True).then_inc(S["tBias"], 1)
            t.wait_ge(S["vRin"], 1)
            t.matmul(psA[:, 3584:3584 + H], ones[0:1, :],
                     bst[0:1, 1088:1088 + H], start=True, stop=True,
                     skip_group_check=True).then_inc(S["tRsc"], 1)

            t.wait_ge(S["vBias"], 8)
            t.wait_ge(S["vRsc"], 1)
            if upto == 0:
                for w in range(NW):
                    wge(t, "vSW", w)
                    wge(t, "vSWr", w + 1)
                    t.matmul(psA[:, 1540 + 2 * w:1542 + 2 * w], ones[0:1, :],
                             bst[0:1, 1120 + 2 * w:1122 + 2 * w],
                             start=True, stop=True,
                             skip_group_check=True).then_inc(S["tSWb"], 1)
                return
            def kv_mms(i):
                t.wait_ge(S["pKN"], i + 1)
                t.wait_ge(S["vVB"], i + 1)
                ins = None
                for h in range(H):
                    ins = t.matmul(
                        psA[:, 3072 + 64 * h:3072 + 64 * h + 64],
                        kn[i % 2][:, (h // 2) * 128:(h // 2) * 128 + 128],
                        vb[i % 2][:, h * HD:(h + 1) * HD],
                        start=(i == 0 and h in (0, 8)),
                        stop=(i == NT - 1 and h == H - 1),
                        skip_group_check=True)
                ins.then_inc(S["tKV"], 1)

            for i in range(NT):
                for mi, (base, gate, sem) in enumerate(
                        [(0, "vQe", "tQ"), (1024, "vKe", "tK"),
                         (2048, "vVB", "tV")]):
                    wge(t, gate, i)
                    if i == 0:
                        t.wait_ge(S["gLoad"], 16 * (9 + mi))
                    ins = None
                    for c in range(8):
                        if i == 0 and mi == 0:
                            t.wait_ge(S["dXT"], 16 * (c + 1))
                        for nh in range(2):
                            ins = t.matmul(
                                psA[:, base + nh * 512: base + nh * 512 + 512],
                                actT[:, c, i * 128:(i + 1) * 128],
                                actU[:, c, mi * 1024 + nh * 512:
                                     mi * 1024 + nh * 512 + 512],
                                start=(c == 0), stop=(c == 7),
                                skip_group_check=True)
                    ins.then_inc(S[sem], 1)
                if i >= 1:
                    kv_mms(i - 1)
            kv_mms(NT - 1)
            t.wait_ge(S["vKe"], NT)
            for w in range(NW):
                wge(t, "vSW", w)
                wge(t, "vSWr", w + 1)
                t.matmul(psA[:, 1540 + 2 * w:1542 + 2 * w], ones[0:1, :],
                         bst[0:1, 1120 + 2 * w:1122 + 2 * w],
                         start=True, stop=True,
                         skip_group_check=True).then_inc(S["tSWb"], 1)

            if upto < 2:
                return
            t.wait_ge(S["vKV"], 1)
            t.wait_ge(S["vQe"], NT)
            t.wait_ge(S["vSW"], NW)
            for s in range(8):
                t.wait_ge(S["dQT"], 128 * (s // 2 + 1))
                for c in range(8):
                    cs = 8 * s + c
                    wge(t, "vOev", cs - 3)
                    ins = None
                    for h in (2 * c, 2 * c + 1):
                        r0 = 64 * (h % 2)
                        ins = t.matmul(
                            psA[r0:r0 + 64, (cs % 4) * 512:(cs % 4) * 512 + 512],
                            kvbf[r0:r0 + 64, h * HD:(h + 1) * HD],
                            actT[r0:r0 + 64, c, s * 512:(s + 1) * 512],
                            start=True, stop=True,
                            skip_group_check=True)
                    ins.then_inc(S["tO"], 1)

            if upto < 3:
                return
            t.wait_ge(S["gLoad"], 16 * 12)
            for i in range(NT):
                wge(t, "vZ1", i - 1)
                if i == 0:
                    t.wait_ge(S["vOev"], 64)
                wge(t, "vOev", 8 * (i // 4 + 1))
                base = 1024 + (i % 2) * 1024
                ins = None
                for c in range(8):
                    for nh in range(2):
                        ins = t.matmul(
                            psA[:, base + nh * 512: base + nh * 512 + 512],
                            actT[:, c, i * 128:(i + 1) * 128],
                            actU[:, c, 3072 + nh * 512: 3072 + nh * 512 + 512],
                            start=(c == 0), stop=False,
                            skip_group_check=True)
                for nh in range(2):
                    ins = t.matmul(
                        psA[:, base + nh * 512: base + nh * 512 + 512],
                        onesb[0:1, :],
                        bias_b[0:1, 3, nh * 512:nh * 512 + 512],
                        start=False, stop=True, skip_group_check=True)
                ins.then_inc(S["tZ"], 1)
                if lvl >= 4:
                    t.wait_ge(S["pXQ"], i + 1)
                    wge(t, "aXQe", i)
                    if i == 0:
                        t.wait_ge(S["vKV"], 1)
                        t.wait_ge(S["vSW"], NW)
                    ins = None
                    for c in range(8):
                        dst = bankbf3(6 + c // 4)
                        ins = t.transpose(dst[:, c % 4, :],
                                          xqbf[i % 2][:, c * 128:(c + 1) * 128],
                                          idn[:])
                    ins.then_inc(S["tXQT"], 1)

            if upto < 4:
                return
            GB = (P3A_GB * 4)[:4]
            UB = (P3A_UB * 4)[:4]
            t.wait_ge(S["vOev"], 64)
            t.wait_ge(S["vZ1"], NT)
            t.wait_ge(S["sX1"], 16 * NT)
            t.wait_ge(S["vKV"], 1)
            t.wait_ge(S["aXQe"], NT)
            for i in range(NT):
                t.wait_ge(S["dXQT"], 16 * (i + 1))
                xv = xqTi[i % 2].rearrange("p (a b) -> p a b", b=128)
                for os in range(8):
                    wge(t, "aGev", 8 * i + os - (len(P3A_GB) - 1))
                    if i == 0:
                        t.wait_ge(S["dWgT"], 128 * (os // 2 + 1))
                    ins = None
                    for c in range(8):
                        ins = t.matmul(bank(GB[os % len(P3A_GB)]), xv[:, c, :],
                                       actT[:, c, os * 512:(os + 1) * 512],
                                       start=(c == 0), stop=(c == 7),
                                       skip_group_check=True)
                    ins.then_inc(S["tG"], 1)
                    wge(t, "vUev", 8 * i + os - (len(P3A_UB) - 1))
                    if i == 0:
                        t.wait_ge(S["dWuT"], 128 * (os // 2 + 1))
                    ins = None
                    for c in range(8):
                        ins = t.matmul(bank(UB[os % len(P3A_UB)]), xv[:, c, :],
                                       actU[:, c, os * 512:(os + 1) * 512],
                                       start=(c == 0), stop=(c == 7),
                                       skip_group_check=True)
                    ins.then_inc(S["tU"], 1)

            if upto < 5:
                return
            t.wait_ge(S["pHQ"], NT)
            t.wait_ge(S["aGev"], 8 * NT)
            t.wait_ge(S["vUev"], 8 * NT)
            for i in range(NT):
                t.wait_ge(S["dHQ"], 16 * (i + 1))
                for r in range(8):
                    wge(t, "aHTe", 8 * i + r - 1)
                    dst = bankbf3(4 + r % 2)
                    ins = None
                    for k in range(4):
                        ins = t.transpose(dst[:, k, :],
                                          hqi[i % 2][:, (4 * r + k) * 128:
                                                     (4 * r + k + 1) * 128],
                                          idn[:])
                    ins.then_inc(S["tTr"], 1)
                t.wait_ge(S["aHTe"], 8 * (i + 1))
                hv = hqT[i % 2].rearrange("p (a b) -> p a b", b=128)
                for nh in range(2):
                    wge(t, "vDev", 2 * i + nh - 1)
                    ins = None
                    for c in range(32):
                        if i == 0 and nh == 0:
                            t.wait_ge(S["dWdT"], 16 * (c + 1))
                        ins = t.matmul(bank(6 + nh), hv[:, c, :],
                                       actTf[:, c * 1024 + nh * 512:
                                             c * 1024 + nh * 512 + 512],
                                       start=(c == 0), stop=(c == 31),
                                       skip_group_check=True)
                    ins.then_inc(S["tD"], 1)

        @block.vector
        def _(v):
            fen = [0]

            def vfence(ins):
                fen[0] += 1
                ins.then_inc(S["vFen"], 1)
                v.wait_ge(S["vFen"], fen[0])

            def vrecip(out, in_, pre_ins=None):
                if pre_ins is not None:
                    vfence(pre_ins)
                r = v.reciprocal(out, in_)
                vfence(r)
                return r
            for j in range(8):
                v.wait_ge(S["tBias"], j + 1)
                v.tensor_copy(bias_b[:, j // 2, (j % 2) * 512:(j % 2) * 512 + 512],
                              bank(j % 2)).then_inc(S["vBias"], 1)
            v.wait_ge(S["aSig"], 1)
            v.tensor_scalar_mul(bst[0:1, 1056:1056 + H],
                                bst[0:1, 1040:1040 + H],
                                LN4096).then_inc(S["vSig"], 1)
            v.wait_ge(S["aExp"], 1)
            vrecip(bst[0:1, 1088:1088 + H], bst[0:1, 1072:1072 + H])
            v.engine_nop().then_inc(S["vRin"], 1)
            v.wait_ge(S["tRsc"], 1)
            v.tensor_copy(rsc[:], psA[:, 3584:3584 + H]).then_inc(S["vRsc"], 1)

            def mean_chunk(j):
                v.wait_ge(S["dWm"], 16 * (j + 1))
                v.tensor_reduce(macc[:, j:j + 1].rearrange("p (a o) -> p a o", o=1),
                                wbuf[j % 2].rearrange("p (a b) -> p a b", a=1),
                                AX.X, ALU.add,
                                apply_absolute_value=True).then_inc(S["vMean"], 1)

            def mred_fin(w):
                v.tensor_reduce(mred[:, w:w + 1].rearrange("p (a o) -> p a o", o=1),
                                macc[:, w * NCH:(w + 1) * NCH]
                                .rearrange("p (a b) -> p a b", a=1),
                                AX.X, ALU.add).then_inc(S["vMred"], 1)

            def sw_fin(w):
                v.wait_ge(S["sMR"], 16 * (2 * w + 2))
                vfence(v.tensor_reduce(bst[0:1, 1104 + w:1105 + w]
                                       .rearrange("p (a o) -> p a o", o=1),
                                       bst[0:1, 1152:1280]
                                       .rearrange("p (a b) -> p a b", a=1),
                                       AX.X, ALU.add))
                vfence(v.tensor_scalar_mul(bst[0:1, 1108 + w:1109 + w],
                                           bst[0:1, 1104 + w:1105 + w], INV_N))
                mx = v.tensor_scalar_max(bst[0:1, 1112 + w:1113 + w],
                                         bst[0:1, 1108 + w:1109 + w], 1e-5)
                vrecip(bst[0:1, 1120 + 2 * w:1121 + 2 * w],
                       bst[0:1, 1112 + w:1113 + w], pre_ins=mx)
                vrecip(bst[0:1, 1121 + 2 * w:1122 + 2 * w],
                       bst[0:1, 1120 + 2 * w:1121 + 2 * w])
                v.engine_nop().then_inc(S["vSWr"], 1)
                v.wait_ge(S["tSWb"], w + 1)
                v.tensor_copy(swuw[:, 2 * w:2 * w + 2],
                              psA[:, 1540 + 2 * w:1542 + 2 * w]
                              ).then_inc(S["vSW"], 1)

            mean_emitted = 0
            fin_emitted = 0
            if upto == 0:
                for j in range(NQC):
                    mean_chunk(j)
                for w in range(NW):
                    mred_fin(w)
                    sw_fin(w)
                return
            for i in range(NT):
                v.wait_ge(S["tQ"], i + 1)
                v.tensor_tensor(qf[i % 2], psA[:, 0:1024], bias_b[:, 0, :],
                                ALU.add).then_inc(S["vQe"], 1)
                v.wait_ge(S["aSQq"], i + 1)
                v.tensor_reduce(rq[i % 2].rearrange("p (a o) -> p a o", o=1),
                                sqq.rearrange("p (a b) -> p a b", b=HD),
                                AX.X, ALU.add).then_inc(S["vRQ"], 1)
                v.wait_ge(S["tK"], i + 1)
                wge(v, "pKN", i - 1)
                v.tensor_tensor(kf[i % 2], psA[:, 1024:2048], bias_b[:, 1, :],
                                ALU.add).then_inc(S["vKe"], 1)
                v.wait_ge(S["aSQk"], i + 1)
                v.tensor_reduce(rk[i % 2].rearrange("p (a o) -> p a o", o=1),
                                sqk.rearrange("p (a b) -> p a b", b=HD),
                                AX.X, ALU.add).then_inc(S["vRK"], 1)
                v.wait_ge(S["tV"], i + 1)
                wge(v, "tKV", i - 1)
                v.tensor_tensor(vb[i % 2], psA[:, 2048:3072], bias_b[:, 2, :],
                                ALU.add).then_inc(S["vVB"], 1)
                v.wait_ge(S["aRTq"], i + 1)
                wge(v, "sQN", 16 * (i - 1))
                mx = v.tensor_scalar_max(nrm, srtq[i % 2], 1e-12)
                vrecip(rinvq, nrm, pre_ins=mx)
                a3, b3 = broadcast_tensor_aps(
                    qf[i % 2].rearrange("p (a b) -> p a b", b=HD),
                    rinvq.rearrange("p (a o) -> p a o", o=1))
                v.tensor_tensor(qn[i % 2].rearrange("p (a b) -> p a b", b=HD),
                                a3, b3, ALU.mult).then_inc(S["vQN"], 1)
                v.wait_ge(S["aRTk"], i + 1)
                wge(v, "tKV", i - 1)
                mx = v.tensor_scalar_max(nrm, srtk[i % 2], 1e-12)
                vrecip(rinvk[i % 2], nrm, pre_ins=mx)
                a3, b3 = broadcast_tensor_aps(
                    kf[i % 2].rearrange("p (a b) -> p a b", b=HD),
                    rinvk[i % 2].rearrange("p (a o) -> p a o", o=1))
                v.tensor_tensor(kn[i % 2].rearrange("p (a b) -> p a b", b=HD),
                                a3, b3, ALU.mult).then_inc(S["pKN"], 1)
                while mean_emitted < NQC and MEAN_TILE[mean_emitted] <= i:
                    mean_chunk(mean_emitted)
                    mean_emitted += 1
                while fin_emitted < NW and FIN_TILE[fin_emitted] <= i:
                    mred_fin(fin_emitted)
                    fin_emitted += 1
            v.wait_ge(S["tKV"], NT)
            ins = None
            for h in range(H):
                r0 = 64 * (h % 2)
                col = 3072 + 64 * h
                ins = v.tensor_scalar_mul(kvbf[r0:r0 + 64, h * HD:(h + 1) * HD],
                                          psA[r0:r0 + 64, col:col + 64],
                                          rsc[r0:r0 + 64, h:h + 1])
            ins.then_inc(S["vKV"], 1)
            for w in range(NW):
                sw_fin(w)
            if upto < 2:
                return
            for s in range(8):
                for c in range(8):
                    cs = 8 * s + c
                    v.wait_ge(S["tO"], cs + 1)
                    v.tensor_copy(actT[:, c, s * 512:(s + 1) * 512],
                                  bank(cs % 4)).then_inc(S["vOev"], 1)
            if upto < 3:
                return
            for i in range(NT):
                base = 1024 + (i % 2) * 1024
                v.wait_ge(S["tZ"], i + 1)
                v.wait_ge(S["dXin"], 16 * (i + 1))
                v.tensor_tensor(z2, psA[:, base:base + 1024], xres[i % 2],
                                ALU.add).then_inc(S["vZ1"], 1)
                if lvl < 2:
                    continue
                v.tensor_reduce(st2c(0).rearrange("p (a o) -> p a o", o=1),
                                z2.rearrange("p (a b) -> p a b", a=1),
                                AX.X, ALU.add)
                v.wait_ge(S["aSQ2"], i + 1)
                vfence(v.tensor_reduce(st2c(1).rearrange("p (a o) -> p a o", o=1),
                                       sq2.rearrange("p (a b) -> p a b", a=1),
                                       AX.X, ALU.add))
                v.tensor_scalar_mul(st2c(2), st2c(0), 1.0 / D)
                vfence(v.tensor_scalar_mul(st2c(3), st2c(1), 1.0 / D))
                v.scalar_tensor_tensor(st2c(5), st2c(2), st2c(2), st2c(3),
                                       ALU.mult,
                                       ALU.subtract).then_inc(S["vVar2"], 1)
                v.wait_ge(S["aRS2"], i + 1)
                wge(v, "sX1", 16 * (i - 1))
                vrecip(st2c(6), st2c(7))
                v.tensor_scalar(x1b[i % 2], z2, st2c(2), st2c(6),
                                ALU.subtract, ALU.mult).then_inc(S["vX1"], 1)
                if lvl < 3:
                    continue
                vfence(v.tensor_reduce(st2c(8).rearrange("p (a o) -> p a o", o=1),
                                       x1b[i % 2].rearrange("p (a b) -> p a b", a=1),
                                       AX.X, ALU.max, apply_absolute_value=True))
                mx = v.tensor_scalar_max(st2c(9), st2c(8), 1e-5)
                vrecip(st2c(11), st2c(9), pre_ins=mx)
                v.tensor_scalar_mul(sxs[:, i:i + 1], st2c(9), 1.0 / 127.0)
                vfence(v.tensor_scalar_mul(st2c(10), st2c(11), 127.0))
                i1 = v.tensor_scalar(sq2, x1b[i % 2], st2c(10), MAGIC,
                                     ALU.mult, ALU.add)
                if lvl < 4:
                    i1.then_inc(S["vXI8"], 1)
                    continue
                wge(v, "tXQT", i)
                v.tensor_scalar_add(xqbf[i % 2], sq2,
                                    -MAGIC).then_inc(S["pXQ"], 1)
                for j in P2B_QUANT.get(i, []):
                    w = j // NCH
                    v.wait_ge(S["dWq"], 16 * (j + 1))
                    v.wait_ge(S["vSW"], w + 1)
                    v.tensor_scalar(wbuf[j % 2], wbuf[j % 2],
                                    swuw[:, 2 * w:2 * w + 1], -1.4375,
                                    ALU.mult, ALU.max)
                    v.tensor_scalar(wbuf[j % 2], wbuf[j % 2], 1.4375, MAGIC,
                                    ALU.min, ALU.add)
                    wge(v, "sWQ", 16 * j)
                    v.tensor_scalar_add(wqbf[0], wbuf[j % 2],
                                        -MAGIC).then_inc(S["pQbf"], 1)
            if upto < 4:
                return
            v.wait_ge(S["sWQ"], 16 * NQC)
            for i in range(NT):
                wge(v, "aGev", 8 * i)
                v.tensor_tensor(sgu[:, 0:1], sxs[:, i:i + 1], swuw[:, 1:2],
                                ALU.mult)
                vfence(v.tensor_tensor(sgu[:, 1:2], sxs[:, i:i + 1],
                                       swuw[:, 3:4], ALU.mult))
                v.engine_nop().then_inc(S["vSG"], 1)
                for os in range(8):
                    v.wait_ge(S["tU"], 8 * i + os + 1)
                    v.wait_ge(S["aGev"], 8 * i + os + 1)
                    ins = v.scalar_tensor_tensor(hbuf[:, os * 512:(os + 1) * 512],
                                                 bank((P3A_UB * 4)[os % len(P3A_UB)]),
                                                 sgu[:, 1:2],
                                                 gbuf[:, os * 512:(os + 1) * 512],
                                                 ALU.mult, ALU.mult)
                    if os == 7:
                        vfence(ins)
                        v.engine_nop().then_inc(S["vUev"], 1)
                    else:
                        ins.then_inc(S["vUev"], 1)
                vfence(v.tensor_reduce(st3c(0).rearrange("p (a o) -> p a o", o=1),
                                       hbuf.rearrange("p (a b) -> p a b", a=1),
                                       AX.X, ALU.max, apply_absolute_value=True))
                mx = v.tensor_scalar_max(st3c(1), st3c(0), 1e-5)
                vrecip(st3c(3), st3c(1), pre_ins=mx)
                v.tensor_scalar_mul(sds[:, i:i + 1], st3c(1), 1.0 / 127.0)
                vfence(v.tensor_scalar_mul(st3c(2), st3c(3), 127.0))
                v.tensor_scalar(gbuf, hbuf, st3c(2), MAGIC,
                                ALU.mult, ALU.add)
                wge(v, "sHQ", 16 * (i - 1))
                v.tensor_scalar_add(hqb[i % 2], gbuf,
                                    -MAGIC).then_inc(S["pHQ"], 1)
            if upto < 5:
                return
            for i in range(NT):
                vfence(v.tensor_tensor(sgu[:, 2:3], sds[:, i:i + 1],
                                       swuw[:, 5:6], ALU.mult))
                v.wait_ge(S["dX1"], 16 * (i + 1))
                for nh in range(2):
                    v.wait_ge(S["tD"], 2 * i + nh + 1)
                    v.scalar_tensor_tensor(z3[:, nh * 512:(nh + 1) * 512],
                                           bank(6 + nh), sgu[:, 2:3],
                                           x1i[i % 2][:, nh * 512:(nh + 1) * 512],
                                           ALU.mult, ALU.add
                                           ).then_inc(S["vDev"], 1)
                v.tensor_reduce(st4c(0).rearrange("p (a o) -> p a o", o=1),
                                z3.rearrange("p (a b) -> p a b", a=1),
                                AX.X, ALU.add)
                v.wait_ge(S["aSQ3"], i + 1)
                vfence(v.tensor_reduce(st4c(1).rearrange("p (a o) -> p a o", o=1),
                                       sq3.rearrange("p (a b) -> p a b", a=1),
                                       AX.X, ALU.add))
                v.tensor_scalar_mul(st4c(2), st4c(0), 1.0 / D)
                vfence(v.tensor_scalar_mul(st4c(3), st4c(1), 1.0 / D))
                vfence(v.tensor_tensor(st4c(4), st4c(2), st4c(2), ALU.mult))
                v.tensor_tensor(st4c(5), st4c(3), st4c(4),
                                ALU.subtract).then_inc(S["vVar3"], 1)
                v.wait_ge(S["aRS3"], i + 1)
                wge(v, "sOUT", 16 * (i - 1))
                vrecip(st4c(6), st4c(7))
                v.tensor_scalar(outb[i % 2], z3, st4c(2), st4c(6),
                                ALU.subtract, ALU.mult).then_inc(S["vOut"], 1)

        @block.scalar
        def _(s):
            s.wait_ge(S["dBias"], 16 * 5)
            s.activation(bst[0:1, 1040:1040 + H], bst[0:1, 1024:1024 + H],
                         AF.Sigmoid).then_inc(S["aSig"], 1)
            s.wait_ge(S["vSig"], 1)
            s.activation(bst[0:1, 1072:1072 + H], bst[0:1, 1056:1056 + H],
                         AF.Exp).then_inc(S["aExp"], 1)

            qwj = [0]

            def qwrites(upto):
                while qwj[0] < upto:
                    j = qwj[0]
                    s.wait_ge(S["pQbf"], j + 1)
                    dst, src = wsb_out(j)
                    s.dma_start(out=dst, in_=src).then_inc(S["sWQ"], 16)
                    qwj[0] += 1

            def mr_bounce():
                for w in range(NW):
                    s.wait_ge(S["vMred"], w + 1)
                    wge(s, "vSWr", w)
                    s.dma_start(out=mrsc[w, :],
                                in_=mred[:, w:w + 1]).then_inc(S["sMR"], 16)
                    s.wait_ge(S["sMR"], 16 * (2 * w + 1))
                    s.dma_start(out=bst[0:1, 1152:1280],
                                in_=mrsc[w:w + 1, :]).then_inc(S["sMR"], 16)

            if upto == 0:
                mr_bounce()
                return
            for i in range(NT):
                s.wait_ge(S["vQe"], i + 1)
                wge(s, "vRQ", i)
                s.activation(sqq, qf[i % 2], AF.Square).then_inc(S["aSQq"], 1)
                s.wait_ge(S["vRQ"], i + 1)
                wge(s, "vQN", i - 1)
                s.activation(srtq[i % 2], rq[i % 2], AF.Sqrt).then_inc(S["aRTq"], 1)
                s.wait_ge(S["vKe"], i + 1)
                wge(s, "vRK", i)
                s.activation(sqk, kf[i % 2], AF.Square).then_inc(S["aSQk"], 1)
                s.wait_ge(S["vRK"], i + 1)
                wge(s, "pKN", i - 1)
                s.activation(srtk[i % 2], rk[i % 2], AF.Sqrt).then_inc(S["aRTk"], 1)
                s.wait_ge(S["vQN"], i + 1)
                s.dma_start(out=qsc[i * 128:(i + 1) * 128, :],
                            in_=qn[i % 2]).then_inc(S["sQN"], 16)
                qwrites(sum(len(P1_QUANT.get(k, [])) for k in range(i + 1)))
            mr_bounce()
            if upto < 3:
                s.wait_ge(S["sQN"], 16 * NT)
                return
            qwrites(N_P1Q + len(P2A_QUANT))
            if dbg and upto >= 2:
                s.wait_ge(S["vKV"], 1)
                s.dma_start(out=dbg_kv, in_=kvbf[:]).then_inc(S["sDBG"], 16)
                s.wait_ge(S["vOev"], 64)
                s.dma_start(out=dbg_ofm, in_=actT[:]).then_inc(S["sDBG"], 16)
                s.wait_ge(S["vSW"], NW)
                s.dma_start(out=dbg_macc, in_=arena[:, 9408:9459]
                            ).then_inc(S["sDBG"], 16)
                s.dma_start(out=dbg_swuw, in_=swuw[:]).then_inc(S["sDBG"], 16)
                s.dma_start(out=dbg_bst, in_=bst[0:1, 1088:1152]
                            ).then_inc(S["sDBG"], 16)
            for i in range(NT):
                if lvl >= 2:
                    s.wait_ge(S["vZ1"], i + 1)
                    wge(s, "pXQ" if lvl >= 4 else
                        ("vXI8" if lvl >= 3 else "vVar2"), i)
                    s.activation(sq2, z2, AF.Square).then_inc(S["aSQ2"], 1)
                    s.wait_ge(S["vVar2"], i + 1)
                    s.activation(st2c(7), st2c(5), AF.Sqrt, bias=cst[:, 1:2],
                                 scale=cst[:, 4:5]).then_inc(S["aRS2"], 1)
                    s.wait_ge(S["vX1"], i + 1)
                    s.dma_start(out=x1sc[i * 128:(i + 1) * 128, :],
                                in_=x1b[i % 2]).then_inc(S["sX1"], 16)
                if lvl >= 4:
                    s.wait_ge(S["tXQT"], i + 1)
                    wge(s, "sXQT", 16 * (i - 1))
                    xv = xqTt[i % 2].rearrange("p (a b) -> p a b", b=128)
                    s.copy(xv[:, 0:4, :], bankbf3(6)[:, 0:4, :])
                    s.copy(xv[:, 4:8, :],
                           bankbf3(7)[:, 0:4, :]).then_inc(S["aXQe"], 1)
                    s.dma_start(out=xqT_sc.transpose([1, 0, 2])[:, :, i * 128:(i + 1) * 128],
                                in_=xqTt[i % 2].rearrange("p (a b) -> p a b", b=128)
                                ).then_inc(S["sXQT"], 16)
                qwrites(N_P1Q + len(P2A_QUANT)
                        + sum(len(P2B_QUANT.get(k, [])) for k in range(i + 1)))
            qwrites(NQC)
            if upto < 4:
                if lvl >= 2:
                    s.wait_ge(S["sX1"], 16 * NT)
                if lvl >= 4:
                    s.wait_ge(S["sXQT"], 16 * NT)
                s.wait_ge(S["sWQ"], 16 * NQC)
                return
            for i in range(NT):
                for os in range(8):
                    s.wait_ge(S["tG"], 8 * i + os + 1)
                    s.wait_ge(S["vSG"], i + 1)
                    if os == 0:
                        wge(s, "pHQ", i)
                    wge(s, "vUev", 8 * (i - 1) + os + 1)
                    s.activation(gbuf[:, os * 512:(os + 1) * 512],
                                 bank((P3A_GB * 4)[os % len(P3A_GB)]), AF.Silu,
                                 scale=sgu[:, 0:1]).then_inc(S["aGev"], 1)
                s.wait_ge(S["pHQ"], i + 1)
                s.dma_start(out=hqsc[i * 128:(i + 1) * 128, :],
                            in_=hqb[i % 2]).then_inc(S["sHQ"], 16)
            if upto < 5:
                s.wait_ge(S["sHQ"], 16 * NT)
                return
            qwrites(NQC)
            def x1_load(i):
                if i == 0:
                    s.wait_ge(S["pHQ"], NT)
                s.wait_ge(S["sX1"], 16 * (i + 1))
                wge(s, "vDev", 2 * (i - 1))
                s.dma_start(out=x1i[i % 2],
                            in_=x1sc[i * 128:(i + 1) * 128, :]
                            ).then_inc(S["dX1"], 16)

            def hq_evacs(i):
                for r in range(8):
                    s.wait_ge(S["tTr"], 8 * i + r + 1)
                    if r == 0:
                        wge(s, "tD", 2 * (i - 1))
                    hv = hqT[i % 2].rearrange("p (a b) -> p a b", b=128)
                    s.copy(hv[:, 4 * r:4 * r + 4, :],
                           bankbf3(4 + r % 2)[:, 0:4, :]).then_inc(S["aHTe"], 1)

            x1_load(0)
            x1_load(1)
            hq_evacs(0)
            for i in range(NT):
                if i + 1 < NT:
                    hq_evacs(i + 1)
                s.wait_ge(S["vDev"], 2 * i + 2)
                wge(s, "vVar3", i)
                s.activation(sq3, z3, AF.Square).then_inc(S["aSQ3"], 1)
                if i + 2 < NT:
                    x1_load(i + 2)
                s.wait_ge(S["vVar3"], i + 1)
                s.activation(st4c(7), st4c(5), AF.Sqrt,
                             bias=cst[:, 1:2]).then_inc(S["aRS3"], 1)
                s.wait_ge(S["vOut"], i + 1)
                s.dma_start(out=out_h[i * 128:(i + 1) * 128, :],
                            in_=outb[i % 2]).then_inc(S["sOUT"], 16)
            s.wait_ge(S["sOUT"], 16 * NT)

        stack.close()
    return nc


_NC_CACHE = []


def kernel(L0_attn_x, L0_attn_Wq, L0_attn_bq, L0_attn_Wk, L0_attn_bk,
           L0_attn_Wv, L0_attn_bv, L0_attn_Wo, L0_attn_bo, L0_attn_m,
           L0_ffn_w_gate, L0_ffn_w_up, L0_ffn_w_down):
    if not _NC_CACHE:
        _NC_CACHE.append(build())
    nc = _NC_CACHE[0]

    def f(a):
        return np.ascontiguousarray(np.asarray(a, dtype=np.float32))

    X = f(L0_attn_x)
    shared = {
        "wq": f(L0_attn_Wq), "wk": f(L0_attn_Wk), "wv": f(L0_attn_Wv),
        "wo": f(L0_attn_Wo), "bq": f(L0_attn_bq), "bk": f(L0_attn_bk),
        "bv": f(L0_attn_bv), "bo": f(L0_attn_bo), "m": f(L0_attn_m),
        "wg": f(L0_ffn_w_gate), "wu": f(L0_ffn_w_up), "wd": f(L0_ffn_w_down),
    }
    in_maps = [dict(shared, x=X[i]) for i in range(NCHIP)]
    res = run_bass_kernel_spmd(nc, in_maps, core_ids=list(range(NCHIP)))
    return np.stack([np.asarray(res.results[i]["out"], dtype=np.float32)
                     for i in range(NCHIP)], axis=0)
